# version 15
# baseline (speedup 1.0000x reference)
import math
from contextlib import ExitStack

import numpy as np

import concourse.bass as bass
import concourse.mybir as mybir
from concourse.bass_utils import run_bass_kernel_spmd

F32 = mybir.dt.float32
BF16 = mybir.dt.bfloat16
U8 = mybir.dt.uint8
AX = mybir.AluOpType
AF = mybir.ActivationFunctionType

NC = 8
H = 16
DH = 128
DM = 2048
KD = DM // 128
SEQ = 8192
TLOC = SEQ // NC
NG = 2
GT = 512
CH = 128
NCHG = GT // CH
NCH = TLOC // CH
HID = 5632
NJ = HID // 128
MODLOC = 12288 // NC
EPS = 1e-5
ZP = 128.0


def _decays():
    start = 2.0 ** (-(2.0 ** (-(math.log2(H) - 3))))
    slopes = np.array([start ** (i + 1) for i in range(H)], dtype=np.float32)
    return np.exp(-slopes.astype(np.float64))


def _host_consts(core):
    lam = _decays()
    pos = np.arange(CH, dtype=np.float64)
    dm = np.zeros((CH, H * CH), dtype=np.float64)
    dq = np.zeros((CH, H * CH), dtype=np.float64)
    dk = np.zeros((CH, H), dtype=np.float64)
    wc = np.zeros((CH, 128), dtype=np.float64)
    for h in range(H):
        M = lam[h] ** np.maximum(pos[None, :] - pos[:, None], 0.0)
        dm[:, h * CH:(h + 1) * CH] = np.where(pos[None, :] >= pos[:, None], M, 0.0)
        dq[:, h * CH:(h + 1) * CH] = (lam[h] ** (pos + 1.0))[None, :]
        dk[:, h] = lam[h] ** (CH - 1.0 - pos)
        for j in range(core):
            wc[:, j * H + h] = lam[h] ** (1024.0 * (core - 1 - j))
    lamc = [float(lam[h] ** CH) for h in range(H)]
    return (dm.astype(np.float32), dq.astype(np.float32),
            dk.astype(np.float32), wc.astype(np.float32), lamc)



def _x_blob(x, core):
    sl = np.asarray(x[0, core * TLOC:(core + 1) * TLOC, :])
    return np.ascontiguousarray(sl.T.reshape(KD, 128, TLOC).transpose(1, 0, 2)
                                .reshape(128, KD * TLOC))


def _lhsT_blob(WT, nf):
    a = WT.reshape(KD, 128, nf, 128).transpose(2, 1, 0, 3)
    return np.ascontiguousarray(a.reshape(nf, 128, KD * 128))


def _wv_blob(WvT):
    a = WvT.reshape(4, 4, 128, 4, 512).transpose(3, 0, 2, 1, 4)
    return np.ascontiguousarray(a.reshape(4, 4, 128, 4 * 512))


def _wd_blob(WdT):
    a = WdT.reshape(NJ, 128, 16, 128).transpose(2, 1, 0, 3)
    return np.ascontiguousarray(a.reshape(16, 128, NJ * 128))


def _wg_blob(WT):
    a = WT.reshape(KD, 128, NJ, 128).transpose(2, 1, 0, 3)
    return np.ascontiguousarray(a.reshape(NJ, 128, KD * 128))


def _wm_blob(wmod, core):
    lo = core * MODLOC
    a = np.asarray(wmod[:, lo:lo + MODLOC]).reshape(KD, 128, 12, 128)
    return np.ascontiguousarray(a.transpose(2, 1, 0, 3).reshape(12, 128, KD * 128))


def _col16(v):
    return np.ascontiguousarray(np.asarray(v).reshape(16, 128).T)


def _col44(v):
    return np.ascontiguousarray(np.asarray(v).reshape(NJ, 128).T)



class Ch:
    def __init__(self, stack, nc, name):
        self.sem = stack.enter_context(nc.semaphore(name))
        self.n = 0

    def inc(self, inst, k=1):
        inst.then_inc(self.sem, k)
        self.n += k
        return self.n


class K:
    def __init__(self, nc, stack):
        self.nc = nc
        self.stack = stack
        self.block = stack.enter_context(nc.Block())
        self.chans = {}
        self.waited = {}
        self.audit = []

    def ch(self, name):
        if name not in self.chans:
            self.chans[name] = Ch(self.stack, self.nc, name)
        return self.chans[name]

    def wait(self, eng, engname, ch, val):
        if val <= 0:
            return
        key = (engname, ch.sem.name)
        if self.waited.get(key, 0) >= val:
            return
        self.audit.append((engname, ch, val))
        eng.wait_ge(ch.sem, val)
        self.waited[key] = val

    def seg(self, engname, fn):
        def wrap(eng):
            fn(eng, engname)
        getattr(self.block, engname)(wrap)


class Bank:

    def __init__(self, K_, ps, name):
        self.K = K_
        self.ps = ps
        self.marks = []

    def begin(self, eng, engname, rel=None):
        if self.marks:
            ch, val = self.marks[-1]
            self.K.wait(eng, engname, ch, val)
        if rel is not None:
            ch, val = rel
            self.K.wait(eng, engname, ch, val)

    def release_at(self, ch, val):
        self.marks.append((ch, val))


class Stream:

    def __init__(self, K_, name, nslots):
        self.K = K_
        self.lds = [K_.ch(f"{name}_ld{i}") for i in range(nslots)]
        self.nslots = nslots
        self.iprod = 0
        self.mark_of = None

    def load(self, eng, engname, dma_fn):
        i = self.iprod
        if i >= self.nslots:
            ch, val = self.mark_of(i - self.nslots)
            self.K.wait(eng, engname, ch, val)
        self.lds[i % self.nslots].inc(dma_fn(), 16)
        self.iprod += 1

    def wait_tile(self, eng, engname, i):
        self.K.wait(eng, engname, self.lds[i % self.nslots],
                    16 * (i // self.nslots + 1))


DEBUG_TAPS = False


def build_program(lamc):
    nc = bass.Bass("TRN2", target_bir_lowering=False,
                   detect_race_conditions=False)

    d_x = nc.dram_tensor("x", [128, KD * TLOC], F32, kind="ExternalInput").ap()
    d_out = nc.dram_tensor("out", [128, KD * TLOC], F32, kind="ExternalOutput").ap()
    d_dm = nc.dram_tensor("dmask", [128, H * CH], F32, kind="ExternalInput").ap()
    d_dq = nc.dram_tensor("dqt", [128, H * CH], F32, kind="ExternalInput").ap()
    d_id = nc.dram_tensor("ident", [128, 128], F32, kind="ExternalInput").ap()
    d_dk = nc.dram_tensor("dk", [128, H], F32, kind="ExternalInput").ap()
    d_wc = nc.dram_tensor("wcomb", [128, 128], F32, kind="ExternalInput").ap()
    d_cc = nc.dram_tensor("cvec", [128, 16], F32, kind="ExternalInput").ap()
    d_bq = nc.dram_tensor("bq", [128, 16], F32, kind="ExternalInput").ap()
    d_bk = nc.dram_tensor("bk", [128, 16], F32, kind="ExternalInput").ap()
    d_bv = nc.dram_tensor("bv", [1, DM], F32, kind="ExternalInput").ap()
    d_bo = nc.dram_tensor("bo", [128, 16], F32, kind="ExternalInput").ap()
    d_sg = nc.dram_tensor("sg", [128, NJ], F32, kind="ExternalInput").ap()
    d_su = nc.dram_tensor("su", [128, NJ], F32, kind="ExternalInput").ap()
    d_sd = nc.dram_tensor("sd", [128, 16], F32, kind="ExternalInput").ap()
    d_wq = nc.dram_tensor("wq", [16, 128, KD * 128], F32, kind="ExternalInput").ap()
    d_wk = nc.dram_tensor("wk", [16, 128, KD * 128], F32, kind="ExternalInput").ap()
    d_wv = nc.dram_tensor("wv", [4, 4, 128, 4 * 512], F32, kind="ExternalInput").ap()
    d_wo = nc.dram_tensor("wo", [16, 128, KD * 128], F32, kind="ExternalInput").ap()
    d_wg = nc.dram_tensor("wg", [NJ, 128, KD * 128], U8, kind="ExternalInput").ap()
    d_wu = nc.dram_tensor("wu", [NJ, 128, KD * 128], U8, kind="ExternalInput").ap()
    d_wd = nc.dram_tensor("wd", [16, 128, NJ * 128], U8, kind="ExternalInput").ap()
    d_wm = nc.dram_tensor("wm", [12, 128, KD * 128], F32, kind="ExternalInput").ap()

    d_mci = nc.dram_tensor("mod_ci", [128, 12], F32)
    d_mco = nc.dram_tensor("mod_co", [NC, 128, 12], F32)
    d_gci = nc.dram_tensor("g_ci", [128, H * CH], BF16)
    d_gco = nc.dram_tensor("g_co", [NC, 128, H * CH], BF16)
    if DEBUG_TAPS:
        dbg = {
            "dbg_modsb": nc.dram_tensor("dbg_modsb", [128, 96], F32, kind="ExternalOutput").ap(),
            "dbg_rows": nc.dram_tensor("dbg_rows", [1, 2048], BF16, kind="ExternalOutput").ap(),
            "dbg_S": nc.dram_tensor("dbg_S", [128, H * CH], F32, kind="ExternalOutput").ap(),
            "dbg_attn": nc.dram_tensor("dbg_attn", [128, H * GT], BF16, kind="ExternalOutput").ap(),
            "dbg_hsp": nc.dram_tensor("dbg_hsp", [128, NG * KD * GT], BF16, kind="ExternalOutput").ap(),
            "dbg_ksp": nc.dram_tensor("dbg_ksp", [128, NG * H * GT], BF16, kind="ExternalOutput").ap(),
            "dbg_vsp": nc.dram_tensor("dbg_vsp", [128, NG * NCHG * DM], BF16, kind="ExternalOutput").ap(),
            "dbg_asp": nc.dram_tensor("dbg_asp", [128, NCH * H * CH], BF16, kind="ExternalOutput").ap(),
            "dbg_gco": nc.dram_tensor("dbg_gco", [NC, 128, H * CH], BF16, kind="ExternalOutput").ap(),
        }
    d_hsp = nc.dram_tensor("h_sp", [128, NG * KD * GT], BF16)
    d_ksp = nc.dram_tensor("k_sp", [128, NG * H * GT], BF16)
    d_vsp = nc.dram_tensor("v_sp", [128, NG * NCHG * DM], BF16)
    d_asp = nc.dram_tensor("a_sp", [128, NCH * H * CH], BF16)

    stack = ExitStack()
    with stack:
        sbt = lambda name, shape, dt: stack.enter_context(nc.sbuf_tensor(name, shape, dt))
        pst = lambda name, shape, dt: stack.enter_context(nc.psum_tensor(name, shape, dt))

        x_sb = sbt("x_sb", [128, KD * TLOC], F32)
        arena = sbt("arena", [128, 24576], BF16)
        q_sb = arena[:, 0:8192]
        k_sb = arena[:, 8192:16384]
        v_sb = arena[:, 16384:24576]
        gp_sb = arena[:, 0:NJ * 512]
        h_sb = sbt("h_sb", [128, KD * GT], BF16)
        attn_sb = sbt("attn_sb", [128, H * GT], BF16)
        wstr = sbt("wstr", [128, 4 * 2048], BF16)
        S_sb = sbt("S_sb", [128, H * CH], F32)
        Sbf = sbt("Sbf", [128, H * CH], BF16)
        dm_bf = sbt("dm_bf", [128, H * CH], BF16)
        dq_bf = sbt("dq_bf", [128, H * CH], BF16)
        id_bf = sbt("id_bf", [128, 128], BF16)
        stg = sbt("stg", [128, 1024], F32)
        t3bf = sbt("t3bf", [128, 512], BF16)
        awin = sbt("awin", [128, H * CH], BF16)
        gall = sbt("gall", [128, 2 * NC * CH], BF16)
        xb_sb = sbt("xb_sb", [128, 512], BF16)
        x2b_sb = sbt("x2b_sb", [128, 512], BF16)
        tb_slots = (t3bf, x2b_sb)
        chb_sb = xb_sb
        rows = sbt("rows", [1, 2048], BF16)
        brow = sbt("brow", [1, DM + 512], BF16)
        scT = sbt("scT", [128, 128], BF16)
        ktil = sbt("ktil", [128, 128], BF16)
        dk_sb = sbt("dk_sb", [128, H], F32)
        wc_sb = sbt("wc_sb", [128, 128], F32)
        cc_sb = sbt("cc_sb", [128, 16], F32)
        siluc = sbt("siluc", [128, 16], F32)
        bq_sb = sbt("bq_sb", [128, 16], F32)
        bk_sb = sbt("bk_sb", [128, 16], F32)
        bo_sb = sbt("bo_sb", [128, 16], F32)
        sg_sb = sbt("sg_sb", [128, NJ], F32)
        su_sb = sbt("su_sb", [128, NJ], F32)
        sd_sb = sbt("sd_sb", [128, 16], F32)
        modsb = sbt("modsb", [128, 96], F32)
        modpart = sbt("modpart", [128, 12], F32)
        bga_sb = sbt("bga_sb", [128, 16], F32)
        sdgb_sb = sbt("sdgb_sb", [128, 16], F32)
        tmpa = sbt("tmpa", [128, 16], F32)
        tmpb = sbt("tmpb", [128, 16], F32)
        onec = sbt("onec", [128, 1], BF16)
        ones_row = brow[0:1, DM:DM + 512]
        bv_row = brow[0:1, 0:DM]
        ch_row = brow[0:1, 0:512]
        cp_row = brow[0:1, 512:1024]

        psA = pst("psA", [128, 512], F32)
        psB = pst("psB", [128, 512], F32)
        psC = pst("psC", [128, 512], F32)
        psD = pst("psD", [128, 512], F32)
        psE = pst("psE", [128, 512], F32)
        psF = pst("psF", [128, 512], F32)
        psG = pst("psG", [128, 512], F32)
        psH = pst("psH", [128, 1024], BF16)

        kk = K(nc, stack)
        W = kk.wait
        bA = Bank(kk, psA, "bA"); bB = Bank(kk, psB, "bB")
        bC = Bank(kk, psC, "bC"); bD = Bank(kk, psD, "bD")
        bE = Bank(kk, psE, "bE"); bF = Bank(kk, psF, "bF")
        bG = Bank(kk, psG, "bG"); bH = Bank(kk, psH, "bH")

        st_wm = Stream(kk, "wm", 2)
        st_wk = Stream(kk, "swk", 2)
        st_wq = Stream(kk, "swq", 2)
        st_wv = Stream(kk, "swv", 2)
        st_wo = Stream(kk, "swo", 2)
        st_wg = Stream(kk, "swg", 4)
        st_wd = Stream(kk, "swd", 4)
        all_streams = [st_wm, st_wk, st_wq, st_wv, st_wo, st_wg, st_wd]

        def guard_streams(e, en, this):
            for s in all_streams:
                if s is this or s.iprod == 0:
                    continue
                best = {}
                for i in range(s.iprod):
                    ch, val = s.mark_of(i)
                    if val > best.get(id(ch), (ch, 0))[1]:
                        best[id(ch)] = (ch, val)
                for ch, val in best.values():
                    W(e, en, ch, val)

        c_x = kk.ch("c_x")
        c_x2 = kk.ch("c_x2")
        c_k = kk.ch("c_k")
        v_zc = kk.ch("v_zc")
        t_chb = kk.ch("t_chb")
        s_chb = kk.ch("s_chb")
        c_cg = kk.ch("c_cg")
        c_cc = kk.ch("c_cc")
        c_ms = kk.ch("c_ms")
        v_ms = kk.ch("v_ms")
        s_sc = kk.ch("s_sc")
        t_mod = kk.ch("t_mod")
        s_mev = kk.ch("s_mev")
        v_mod = kk.ch("v_mod")
        t_st = kk.ch("t_st")
        s_xb = kk.ch("s_xb")
        v_x2 = kk.ch("v_x2")
        v_sev = kk.ch("v_sev")
        s_sq = kk.ch("s_sq")
        v_rc = kk.ch("v_rc")
        t_mr = kk.ch("t_mr")
        v_h1 = kk.ch("v_h1")
        s_h = kk.ch("s_h")
        t_kp = kk.ch("t_kp")
        s_kp = kk.ch("s_kp")
        t_vk = kk.ch("t_vk")
        s_vp = kk.ch("s_vp")
        t_qp = kk.ch("t_qp")
        s_qp = kk.ch("s_qp")
        t_kt = kk.ch("t_kt")
        s_kt = kk.ch("s_kt")
        t_am = kk.ch("t_am")
        v_su = kk.ch("v_su")
        s_ae = kk.ch("s_ae")
        c_sph = kk.ch("c_sph")
        c_spk = kk.ch("c_spk")
        c_spv = kk.ch("c_spv")
        c_spa = kk.ch("c_spa")
        c_gi = kk.ch("c_gi")
        c_l2h = kk.ch("c_l2h")
        c_l2k = kk.ch("c_l2k")
        c_l2v = kk.ch("c_l2v")
        s_gc = kk.ch("s_gc")
        c_gw0 = kk.ch("c_gw0")
        c_gw1 = kk.ch("c_gw1")
        v_cb = kk.ch("v_cb")
        c_aw0 = kk.ch("c_aw0")
        c_aw1 = kk.ch("c_aw1")
        t_sc2 = kk.ch("t_sc2")
        v_mk = kk.ch("v_mk")
        t_ia = kk.ch("t_ia")
        v_mg = kk.ch("v_mg")
        s_cast = kk.ch("s_cast")
        t_op = kk.ch("t_op")
        t_dq3 = kk.ch("t_dq3")
        s_op = kk.ch("s_op")
        v_rs = kk.ch("v_rs")
        t_ff = kk.ch("t_ff")
        s_fg = kk.ch("s_fg")
        s_ff = kk.ch("s_ff")
        v_ff = kk.ch("v_ff")
        t_cs = kk.ch("t_cs")
        v_cs = kk.ch("v_cs")
        c_out = kk.ch("c_out")

        tok = {}

        st_wm.mark_of = lambda i: (t_mod, i + 1)
        st_wk.mark_of = lambda i: (t_kp, i + 1)
        st_wq.mark_of = lambda i: (t_qp, i + 1)
        st_wv.mark_of = lambda i: (t_vk, i + 1)
        st_wo.mark_of = lambda i: (t_op, i + 1)
        st_wg.mark_of = lambda i: (t_ff, i // 2 + 1)
        st_wd.mark_of = lambda i: (
            (t_op, 32 + (i // 64) * 16 + (i % 64) // 4 + 1)
            if i % 4 == 3 else
            (t_dq3, (i // 64) * 48 + ((i % 64) // 4) * 3 + (i % 4) + 1))

        def p0_sync(e, en):
            wm32 = arena[:, 0:8192].bitcast(F32)
            for dst, src in ((cc_sb, d_cc), (dk_sb, d_dk), (wc_sb, d_wc),
                             (bq_sb, d_bq), (bk_sb, d_bk), (bo_sb, d_bo),
                             (sg_sb, d_sg), (su_sb, d_su), (sd_sb, d_sd)):
                c_k.inc(e.dma_start(out=dst[:, :], in_=src), 16)
            tok["ck_all"] = c_k.n
            for fc in range(2):
                st_wm.load(e, en, lambda fc=fc: e.dma_start(
                    out=wm32[:, (fc % 2) * 2048:(fc % 2) * 2048 + 2048],
                    in_=d_wm[fc]))
            x3 = x_sb[:, :].rearrange("p (f t) -> p f t", f=KD)
            d3 = d_x.rearrange("p (f t) -> p f t", f=KD)
            c_x.inc(e.dma_start(out=x3[:, :, 0:GT], in_=d3[:, :, 0:GT]), 16)
            c_x2.inc(e.dma_start(out=x3[:, :, GT:TLOC], in_=d3[:, :, GT:TLOC]), 16)
            tok["cx_all"] = c_x.n
            for fc in range(2, 12):
                st_wm.load(e, en, lambda fc=fc: e.dma_start(
                    out=wm32[:, (fc % 2) * 2048:(fc % 2) * 2048 + 2048],
                    in_=d_wm[fc]))
        kk.seg("sync", p0_sync)

        def p0_gp(e, en):
            c_cg.inc(e.dma_start(out=dm_bf[:, :], in_=d_dm), 16)
            c_cg.inc(e.dma_start(out=dq_bf[:, :], in_=d_dq), 16)
            c_cg.inc(e.dma_start(out=id_bf[:, :], in_=d_id), 16)
            c_cg.inc(e.dma_start(out=bv_row, in_=d_bv), 16)
            tok["cg_all"] = c_cg.n
        kk.seg("gpsimd", p0_gp)

        def p0_vec(e, en):
            e.memset(S_sb[:, :], 0.0)
            e.memset(onec[:, :], 1.0)
            v_ms.inc(e.memset(ones_row, 1.0), 1)
        kk.seg("vector", p0_vec)

        def p0_scal(e, en):
            W(e, en, c_k, tok["ck_all"])
            s_sc.inc(e.activation(siluc[:, :], cc_sb[:, :], AF.Silu), 1)
        kk.seg("scalar", p0_scal)

        def p1_tensor(e, en):
            W(e, en, s_sc, 1)
            wm32 = arena[:, 0:8192].bitcast(F32)
            for fc in range(12):
                st_wm.wait_tile(e, en, fc)
                b = (bA, bB)[fc % 2]
                b.begin(e, en, rel=(s_mev, fc - 1) if fc >= 2 else None)
                sl = wm32[:, (fc % 2) * 2048:(fc % 2) * 2048 + 2048]
                for kdi in range(KD):
                    inst = e.matmul(b.ps[:, 0:1], sl[:, kdi * 128:(kdi + 1) * 128],
                                    siluc[:, kdi:kdi + 1],
                                    start=(kdi == 0), stop=(kdi == KD - 1))
                t_mod.inc(inst, 1)
        kk.seg("tensor", p1_tensor)

        def p1_scal(e, en):
            for fc in range(12):
                W(e, en, t_mod, fc + 1)
                b = (bA, bB)[fc % 2]
                inst = e.activation(modpart[:, fc:fc + 1], b.ps[:, 0:1], AF.Identity)
                s_mev.inc(inst, 1)
                b.release_at(s_mev, s_mev.n)
        kk.seg("scalar", p1_scal)

        def p1_sync(e, en):
            W(e, en, s_mev, 12)
            c_ms.inc(e.dma_start(out=d_mci.ap(), in_=modpart[:, :]), 16)
        kk.seg("sync", p1_sync)

        def p1_gp(e, en):
            W(e, en, c_ms, 16)
            inst = e.collective_compute(
                "AllGather", AX.bypass, replica_groups=[list(range(NC))],
                ins=[d_mci.ap().opt()], outs=[d_mco.ap().opt()])
            c_cc.inc(inst, 1)
        kk.seg("gpsimd", p1_gp)

        def p1b_sync(e, en):
            W(e, en, c_cc, 1)
            c_ms.inc(e.dma_start(
                out=modsb[:, :].rearrange("p (r c) -> p r c", r=NC),
                in_=d_mco.ap().transpose([1, 0, 2])), 16)
        kk.seg("sync", p1b_sync)

        def p1_vec(e, en):
            vb = kk.ch("v_bar")

            def bar(inst):
                vb.inc(inst, 1)
                e.wait_ge(vb.sem, vb.n)
            W(e, en, c_ms, 32)
            e.tensor_scalar_add(modsb[:, 16:32], modsb[:, 16:32], 1.0)
            e.tensor_scalar_add(modsb[:, 64:80], modsb[:, 64:80], 1.0)
            for base in (32, 80):
                g0 = modsb[:, base:base + 16]
                bar(e.tensor_scalar_mul(tmpa[:, :], g0, -1.0))
                bar(e.tensor_max(tmpb[:, :], g0, tmpa[:, :]))
                bar(e.tensor_scalar_max(tmpb[:, :], tmpb[:, :], 0.25))
                bar(e.tensor_scalar(tmpa[:, :], g0, 0.0, None, AX.is_ge))
                bar(e.tensor_scalar_mul(tmpa[:, :], tmpa[:, :], 2.0))
                bar(e.tensor_scalar_add(tmpa[:, :], tmpa[:, :], -1.0))
                bar(e.tensor_mul(g0, tmpb[:, :], tmpa[:, :]))
            W(e, en, c_k, tok["ck_all"])
            e.tensor_mul(bga_sb[:, :], modsb[:, 32:48], bo_sb[:, :])
            inst = e.tensor_mul(sdgb_sb[:, :], modsb[:, 80:96], sd_sb[:, :])
            v_mod.inc(inst, 1)
        kk.seg("vector", p1_vec)

        def ln_stats():
            vrs_pre = v_rs.n
            base_sev0 = v_sev.n
            for g in range(NG):
                base_t = t_st.n
                base_sx = s_xb.n
                base_v2 = v_x2.n
                base_sev = v_sev.n
                base_sq = s_sq.n

                def f_scal(e, en, g=g, base_t=base_t, base_sx=base_sx):
                    W(e, en, (c_x, c_x2)[g], 16)
                    W(e, en, v_rs, vrs_pre)
                    for f in range(KD):
                        if g * KD + f > 0:
                            W(e, en, t_st, base_t + f)
                        xs = x_sb[:, f * TLOC + g * GT: f * TLOC + g * GT + GT]
                        s_xb.inc(e.activation(xb_sb[:, :], xs, AF.Identity), 1)
                kk.seg("scalar", f_scal)

                def f_vec(e, en, g=g, base_t=base_t, base_sx=base_sx):
                    for f in range(KD):
                        W(e, en, s_xb, base_sx + f + 1)
                        if g * KD + f > 0:
                            W(e, en, t_st, base_t + f)
                        v_x2.inc(e.tensor_mul(x2b_sb[:, :], xb_sb[:, :], xb_sb[:, :]), 1)
                kk.seg("vector", f_vec)

                def f_tens(e, en, g=g, base_t=base_t, base_sx=base_sx,
                           base_v2=base_v2, base_sev0=base_sev0):
                    W(e, en, v_ms, 1)
                    bA.begin(e, en, rel=(v_sev, base_sev0 + 1) if g else None)
                    bB.begin(e, en, rel=(v_sev, base_sev0 + 2) if g else None)
                    for f in range(KD):
                        W(e, en, s_xb, base_sx + f + 1)
                        e.matmul(psA[0:1, :], onec[:, :], xb_sb[:, :],
                                 start=(f == 0), stop=(f == KD - 1))
                        W(e, en, v_x2, base_v2 + f + 1)
                        inst = e.matmul(psB[0:1, :], onec[:, :], x2b_sb[:, :],
                                        start=(f == 0), stop=(f == KD - 1))
                        t_st.inc(inst, 1)
                kk.seg("tensor", f_tens)

                def f_vec2(e, en, g=g, base_t=base_t):
                    vb = kk.ch("v_bar")
                    W(e, en, t_st, base_t + KD)
                    m_r = rows[0:1, g * GT: g * GT + GT]
                    ssq_f = stg[0:1, 0:512]
                    msq_f = stg[0:1, 512:1024]
                    i1 = e.tensor_scalar_mul(m_r, psA[0:1, :], 1.0 / DM)
                    v_sev.inc(i1, 1)
                    bA.release_at(v_sev, v_sev.n)
                    i2 = e.tensor_scalar_mul(ssq_f, psB[0:1, :], 1.0 / DM)
                    v_sev.inc(i2, 1)
                    bB.release_at(v_sev, v_sev.n)
                    e.wait_ge(v_sev.sem, v_sev.n)
                    vb.inc(e.tensor_mul(msq_f, m_r, m_r), 1)
                    e.wait_ge(vb.sem, vb.n)
                    vb.inc(e.tensor_sub(ssq_f, ssq_f, msq_f), 1)
                    e.wait_ge(vb.sem, vb.n)
                    inst = e.tensor_scalar_add(ssq_f, ssq_f, EPS)
                    v_sev.inc(inst, 1)
                kk.seg("vector", f_vec2)

                def f_sq(e, en, g=g, base_sev=base_sev):
                    W(e, en, v_sev, base_sev + 3)
                    s_sq.inc(e.activation(stg[0:1, 0:512], stg[0:1, 0:512], AF.Sqrt), 1)
                kk.seg("scalar", f_sq)

                def f_rc(e, en, g=g, base_sq=base_sq):
                    vb = kk.ch("v_bar")
                    W(e, en, s_sq, base_sq + 1)
                    vb.inc(e.reciprocal(stg[0:1, 0:512], stg[0:1, 0:512]), 1)
                    e.wait_ge(vb.sem, vb.n)
                    r_r = rows[0:1, 1024 + g * GT: 1024 + g * GT + GT]
                    v_rc.inc(e.tensor_copy(r_r, stg[0:1, 0:512]), 1)
                kk.seg("vector", f_rc)

        def h_pass(g, sa_base, ca_base):
            vrc_pre = v_rc.n
            csp_pre = c_sph.n
            h_guards = [(t_vk, t_vk.n), (t_kp, t_kp.n), (t_qp, t_qp.n),
                        (t_ff, t_ff.n), (t_cs, t_cs.n)]

            def f_tens(e, en):
                W(e, en, v_rc, vrc_pre)
                W(e, en, v_ms, 1)
                bA.begin(e, en)
                e.matmul(psA[:, :], ones_row[0:1, 0:128],
                         rows[0:1, g * GT: g * GT + GT], start=True, stop=True)
                bB.begin(e, en)
                inst = e.matmul(psB[:, :], ones_row[0:1, 0:128],
                                rows[0:1, 1024 + g * GT: 1024 + g * GT + GT],
                                start=True, stop=True)
                t_mr.inc(inst, 1)
            kk.seg("tensor", f_tens)

            base_vh = v_h1.n
            base_sh = s_h.n
            tmr = t_mr.n

            def f_vec(e, en):
                W(e, en, t_mr, tmr)
                for f in range(KD):
                    if f > 0:
                        W(e, en, s_h, base_sh + f)
                    xs = x_sb[:, f * TLOC + g * GT: f * TLOC + g * GT + GT]
                    t1 = stg[:, 0:512]
                    e.scalar_tensor_tensor(t1, xs, 0.0, psA[:, :], AX.add, AX.subtract)
                    inst = e.tensor_mul(t1, t1, psB[:, :])
                    v_h1.inc(inst, 1)
                    if f == KD - 1:
                        bA.release_at(v_h1, v_h1.n)
                        bB.release_at(v_h1, v_h1.n)
            kk.seg("vector", f_vec)

            def f_scal(e, en):
                W(e, en, v_mod, 1)
                W(e, en, c_sph, csp_pre)
                for chn, val in h_guards:
                    W(e, en, chn, val)
                for f in range(KD):
                    W(e, en, v_h1, base_vh + f + 1)
                    inst = e.activation(h_sb[:, f * GT:(f + 1) * GT], stg[:, 0:512],
                                        AF.Identity,
                                        bias=modsb[:, sa_base + f: sa_base + f + 1],
                                        scale=modsb[:, ca_base + f: ca_base + f + 1])
                    s_h.inc(inst, 1)
            kk.seg("scalar", f_scal)

        def kv_group(g):
            def f_gp(e, en):
                guard_streams(e, en, st_wk)
                for f in range(KD):
                    st_wk.load(e, en, lambda f=f: e.dma_start(
                        out=wstr[:, (f % 2) * 2048:(f % 2) * 2048 + 2048],
                        in_=d_wk[f]))
            kk.seg("gpsimd", f_gp)

            base_skp = s_kp.n
            sh_full = s_h.n
            csp_k = c_spk.n

            def f_tens(e, en):
                for f in range(KD):
                    ti = g * KD + f
                    st_wk.wait_tile(e, en, ti)
                    b = (bE, bF)[f % 2]
                    b.begin(e, en, rel=(s_kp, base_skp + f - 1) if f >= 2 else None)
                    sl = wstr[:, (f % 2) * 2048:(f % 2) * 2048 + 2048]
                    for kdi in range(KD):
                        if f == 0:
                            W(e, en, s_h, sh_full - KD + kdi + 1)
                        inst = e.matmul(b.ps[:, :], sl[:, kdi * 128:(kdi + 1) * 128],
                                        h_sb[:, kdi * GT:(kdi + 1) * GT],
                                        start=(kdi == 0), stop=(kdi == KD - 1))
                    t_kp.inc(inst, 1)
            kk.seg("tensor", f_tens)

            def f_scal(e, en):
                W(e, en, c_k, tok["ck_all"])
                W(e, en, c_spk, csp_k)
                for f in range(KD):
                    W(e, en, t_kp, g * KD + f + 1)
                    b = (bE, bF)[f % 2]
                    inst = e.activation(k_sb[:, f * GT:(f + 1) * GT], b.ps[:, :],
                                        AF.Identity, bias=bk_sb[:, f:f + 1])
                    s_kp.inc(inst, 1)
                    b.release_at(s_kp, s_kp.n)
            kk.seg("scalar", f_scal)

            def f_sync(e, en):
                W(e, en, s_kp, base_skp + KD)
                c_spk.inc(e.dma_start(out=d_ksp.ap()[:, g * 8192:(g + 1) * 8192],
                                      in_=k_sb[:, :]), 16)
            kk.seg("sync", f_sync)

            def f_gp2(e, en):
                guard_streams(e, en, st_wv)
                for vg in range(4):
                    for kg in range(4):
                        sl = 2 + (vg * 4 + kg) % 2
                        st_wv.load(e, en, lambda vg=vg, kg=kg, sl=sl: e.dma_start(
                            out=wstr[:, 2048 * sl:2048 * sl + 2048],
                            in_=d_wv[vg, kg]))
            kk.seg("gpsimd", f_gp2)

            base_svp = s_vp.n
            csp_v = c_spv.n

            def f_tens2(e, en):
                vbanks = (bA, bB, bC, bD)
                W(e, en, c_cg, tok["cg_all"])
                W(e, en, v_ms, 1)
                for vg in range(4):
                    for ci, b in enumerate(vbanks):
                        b.begin(e, en, rel=(s_vp, base_svp + (vg - 1) * 4 + ci + 1)
                                if vg >= 1 else None)
                    for c in range(NCHG):
                        e.matmul(vbanks[c].ps[:, :], ones_row[0:1, 0:128],
                                 bv_row[0:1, vg * 512:(vg + 1) * 512],
                                 start=True, stop=False)
                    for kdi in range(KD):
                        ti = g * 16 + vg * 4 + kdi // 4
                        st_wv.wait_tile(e, en, ti)
                        sl = 2 + (vg * 4 + kdi // 4) % 2
                        wsl = wstr[:, 2048 * sl + (kdi % 4) * 512:
                                   2048 * sl + (kdi % 4) * 512 + 512]
                        for c in range(NCHG):
                            inst = e.matmul(
                                vbanks[c].ps[:, :],
                                h_sb[:, kdi * GT + c * CH: kdi * GT + c * CH + CH],
                                wsl, start=False,
                                stop=(kdi == KD - 1))
                        if kdi % 4 == 3:
                            t_vk.inc(inst, 1)
            kk.seg("tensor", f_tens2)

            def f_scal2(e, en):
                vbanks = (bA, bB, bC, bD)
                W(e, en, c_spv, csp_v)
                for vg in range(4):
                    W(e, en, t_vk, g * 16 + (vg + 1) * 4)
                    for c in range(NCHG):
                        inst = e.activation(
                            v_sb[:, c * DM + vg * 512: c * DM + vg * 512 + 512],
                            vbanks[c].ps[:, :], AF.Identity)
                        s_vp.inc(inst, 1)
                        vbanks[c].release_at(s_vp, s_vp.n)
            kk.seg("scalar", f_scal2)

            def f_sync2(e, en):
                W(e, en, s_vp, base_svp + 16)
                c_spv.inc(e.dma_start(out=d_vsp.ap()[:, g * 8192:(g + 1) * 8192],
                                      in_=v_sb[:, :]), 16)
            kk.seg("sync", f_sync2)

            if g == NG - 1:
                def f_hreload(e, en):
                    W(e, en, t_kp, NG * KD)
                    W(e, en, t_vk, NG * 16)
                    W(e, en, c_sph, NG * 16)
                    c_l2h.inc(e.dma_start(out=h_sb[:, :],
                                          in_=d_hsp.ap()[:, 0:8192]), 16)
                kk.seg("sync", f_hreload)

            for c in range(NCHG):
                cg = g * NCHG + c
                csp_aw = c_spa.n

                def f_tens3(e, en, c=c, cg=cg):
                    W(e, en, s_kp, (g + 1) * KD)
                    W(e, en, s_vp, (g + 1) * 16)
                    W(e, en, c_cg, tok["cg_all"])
                    abanks = (bE, bF, bG)
                    for h in range(H):
                        ai = cg * H + h
                        bH.begin(e, en, rel=(s_kt, ai) if ai >= 1 else None)
                        inst = e.transpose(
                            psH[:, 0:128],
                            k_sb[:, h * GT + c * CH: h * GT + c * CH + CH],
                            id_bf[:, :])
                        t_kt.inc(inst, 1)
                        W(e, en, s_kt, cg * H + h + 1)
                        b = abanks[h % 3]
                        b.begin(e, en, rel=(v_su, ai - 2) if ai >= 3 else None)
                        inst = e.matmul(b.ps[:, 0:128], ktil[:, :],
                                        v_sb[:, c * DM + h * CH: c * DM + h * CH + CH],
                                        start=True, stop=True)
                        t_am.inc(inst, 1)
                kk.seg("tensor", f_tens3)

                def f_scal3(e, en, c=c, cg=cg, csp_aw=csp_aw):
                    W(e, en, c_k, tok["ck_all"])
                    W(e, en, c_spa, csp_aw)
                    abanks = (bE, bF, bG)
                    for h in range(H):
                        W(e, en, t_kt, cg * H + h + 1)
                        inst = e.activation(ktil[:, :], psH[:, 0:128], AF.Identity,
                                            scale=dk_sb[:, h:h + 1])
                        s_kt.inc(inst, 1)
                        bH.release_at(s_kt, s_kt.n)
                        W(e, en, t_am, cg * H + h + 1)
                        inst = e.activation(awin[:, h * CH:(h + 1) * CH],
                                            abanks[h % 3].ps[:, 0:128], AF.Identity)
                        s_ae.inc(inst, 1)
                kk.seg("scalar", f_scal3)

                def f_vec3(e, en, c=c, cg=cg):
                    abanks = (bE, bF, bG)
                    for h in range(H):
                        W(e, en, t_am, cg * H + h + 1)
                        W(e, en, s_ae, cg * H + h + 1)
                        b = abanks[h % 3]
                        inst = e.scalar_tensor_tensor(
                            S_sb[:, h * CH:(h + 1) * CH],
                            S_sb[:, h * CH:(h + 1) * CH], lamc[h],
                            b.ps[:, 0:128], AX.mult, AX.add)
                        v_su.inc(inst, 1)
                        b.release_at(v_su, v_su.n)
                kk.seg("vector", f_vec3)

                def f_sync3(e, en, c=c, cg=cg):
                    W(e, en, s_ae, (cg + 1) * H)
                    c_spa.inc(e.dma_start(
                        out=d_asp.ap()[:, cg * 2048:(cg + 1) * 2048],
                        in_=awin[:, :]), 16)
                kk.seg("sync", f_sync3)

        ln_stats()
        for g in range(NG):
            h_pass(g, sa_base=0, ca_base=16)

            def f_hspill(e, en, g=g, sh=None):
                W(e, en, s_h, s_h.n)
                c_sph.inc(e.dma_start(out=d_hsp.ap()[:, g * 8192:(g + 1) * 8192],
                                      in_=h_sb[:, :]), 16)
            kk.seg("sync", f_hspill)
            kv_group(g)

        def f_kvreload(e, en):
            W(e, en, t_kt, NCH * H)
            W(e, en, c_spk, NG * 16)
            c_l2k.inc(e.dma_start(out=k_sb[:, :], in_=d_ksp.ap()[:, 0:8192]), 16)
            W(e, en, t_am, NCH * H)
            W(e, en, c_spv, NG * 16)
            c_l2v.inc(e.dma_start(out=v_sb[:, :], in_=d_vsp.ap()[:, 0:8192]), 16)
        kk.seg("sync", f_kvreload)

        def f_gcast(e, en):
            W(e, en, v_su, NCH * H)
            s_gc.inc(e.activation(Sbf[:, :], S_sb[:, :], AF.Identity), 1)
        kk.seg("scalar", f_gcast)

        def f_gout(e, en):
            W(e, en, s_gc, 1)
            c_gi.inc(e.dma_start(out=d_gci.ap(), in_=Sbf[:, :]), 16)
        kk.seg("sync", f_gout)

        def f_gag(e, en):
            W(e, en, c_gi, 16)
            inst = e.collective_compute(
                "AllGather", AX.bypass, replica_groups=[list(range(NC))],
                ins=[d_gci.ap().opt()], outs=[d_gco.ap().opt()])
            c_cc.inc(inst, 1)
        kk.seg("gpsimd", f_gag)

        def f_gwin(e, en):
            W(e, en, c_cc, 2)
            for h in range(H):
                if h >= 2:
                    W(e, en, v_cb, (h - 1) * NC)
                (c_gw0, c_gw1)[h % 2].inc(e.dma_start(
                    out=gall[:, (h % 2) * 1024:(h % 2) * 1024 + 1024]
                    .rearrange("p (r c) -> p r c", r=NC),
                    in_=d_gco.ap().transpose([1, 0, 2])[:, :, h * CH:(h + 1) * CH]), 16)
        kk.seg("sync", f_gwin)

        def f_comb(e, en):
            W(e, en, c_k, tok["ck_all"])
            for h in range(H):
                W(e, en, (c_gw0, c_gw1)[h % 2], 16 * (h // 2 + 1))
                gw = gall[:, (h % 2) * 1024:(h % 2) * 1024 + 1024]
                ss = S_sb[:, h * CH:(h + 1) * CH]
                inst = e.tensor_scalar(ss, gw[:, 0:128], wc_sb[:, h:h + 1],
                                       None, AX.mult)
                v_cb.inc(inst, 1)
                for j in range(1, NC):
                    inst = e.scalar_tensor_tensor(
                        ss, gw[:, j * CH:(j + 1) * CH],
                        wc_sb[:, j * H + h:j * H + h + 1], ss, AX.mult, AX.add)
                    v_cb.inc(inst, 1)
        kk.seg("vector", f_comb)

        def f_scast0(e, en):
            for h in range(H):
                W(e, en, v_cb, (h + 1) * NC)
                s_cast.inc(e.activation(Sbf[:, h * CH:(h + 1) * CH],
                                        S_sb[:, h * CH:(h + 1) * CH],
                                        AF.Identity), 1)
        kk.seg("scalar", f_scast0)

        scan = {"sc2": 0, "ia": 0}

        def pass2_group(g):
            spa_all = c_spa.n
            sph_all = c_sph.n
            spk_all = c_spk.n
            spv_all = c_spv.n
            tqp_pre = t_qp.n
            tsc_pre = scan["sc2"]
            tia_pre = scan["ia"]

            def f_ld(e, en):
                W(e, en, t_qp, tqp_pre)
                W(e, en, t_sc2, tsc_pre)
                W(e, en, t_ia, tia_pre)
                c_l2h.inc(e.dma_start(out=h_sb[:, :],
                                      in_=d_hsp.ap()[:, g * 8192:(g + 1) * 8192]), 16)
                c_l2k.inc(e.dma_start(out=k_sb[:, :],
                                      in_=d_ksp.ap()[:, g * 8192:(g + 1) * 8192]), 16)
                c_l2v.inc(e.dma_start(out=v_sb[:, :],
                                      in_=d_vsp.ap()[:, g * 8192:(g + 1) * 8192]), 16)
            if g > 0:
                kk.seg("sync", f_ld)

            def f_gp(e, en):
                guard_streams(e, en, st_wq)
                for f in range(KD):
                    st_wq.load(e, en, lambda f=f: e.dma_start(
                        out=wstr[:, (f % 2) * 2048:(f % 2) * 2048 + 2048],
                        in_=d_wq[f]))
            kk.seg("gpsimd", f_gp)

            def f_tens(e, en):
                W(e, en, c_l2h, (g + 1) * 16)
                for f in range(KD):
                    ti = g * KD + f
                    st_wq.wait_tile(e, en, ti)
                    b = (bE, bF)[f % 2]
                    b.begin(e, en, rel=(s_qp, g * KD + f - 1) if f >= 2 else None)
                    sl = wstr[:, (f % 2) * 2048:(f % 2) * 2048 + 2048]
                    for kdi in range(KD):
                        inst = e.matmul(b.ps[:, :], sl[:, kdi * 128:(kdi + 1) * 128],
                                        h_sb[:, kdi * GT:(kdi + 1) * GT],
                                        start=(kdi == 0), stop=(kdi == KD - 1))
                    t_qp.inc(inst, 1)
            kk.seg("tensor", f_tens)

            def f_scal(e, en):
                W(e, en, c_k, tok["ck_all"])
                for f in range(KD):
                    W(e, en, t_qp, g * KD + f + 1)
                    b = (bE, bF)[f % 2]
                    inst = e.activation(q_sb[:, f * GT:(f + 1) * GT], b.ps[:, :],
                                        AF.Identity, bias=bq_sb[:, f:f + 1])
                    s_qp.inc(inst, 1)
                    b.release_at(s_qp, s_qp.n)
            kk.seg("scalar", f_scal)

            for c in range(NCHG):
                cg = g * NCHG + c

                def f_aw(e, en, c=c, cg=cg, spa_all=spa_all):
                    if cg == 0:
                        W(e, en, c_spa, NCH * 16)
                    if cg == 1:
                        W(e, en, v_cb, H * NC)
                    if cg >= 2:
                        W(e, en, v_su, NCH * H + (cg - 1) * H)
                    dst = awin if cg % 2 == 0 else gall[:, 0:2048]
                    (c_aw0, c_aw1)[cg % 2].inc(e.dma_start(
                        out=dst[:, :],
                        in_=d_asp.ap()[:, cg * 2048:(cg + 1) * 2048]), 16)
                kk.seg("sync", f_aw)

                def f_tens2(e, en, c=c, cg=cg, g=g):
                    W(e, en, s_qp, (g + 1) * KD)
                    W(e, en, c_l2k, (g + 1) * 16)
                    W(e, en, c_l2v, (g + 1) * 16)
                    for h in range(H):
                        si = cg * H + h
                        bC.begin(e, en, rel=(v_mk, si) if si >= 1 else None)
                        inst = e.matmul(
                            psC[:, 0:128],
                            k_sb[:, h * GT + c * CH: h * GT + c * CH + CH],
                            q_sb[:, h * GT + c * CH: h * GT + c * CH + CH],
                            start=True, stop=True)
                        t_sc2.inc(inst, 1)
                        W(e, en, s_cast, (h + 1) if cg == 0 else (H + cg))
                        bD.begin(e, en, rel=(v_mg, si) if si >= 1 else None)
                        e.matmul(psD[:, 0:128], Sbf[:, h * CH:(h + 1) * CH],
                                 q_sb[:, h * GT + c * CH: h * GT + c * CH + CH],
                                 start=True, stop=True)
                        W(e, en, v_mk, cg * H + h + 1)
                        bG.begin(e, en, rel=(v_mg, si) if si >= 1 else None)
                        inst = e.matmul(
                            psG[:, 0:128],
                            v_sb[:, c * DM + h * CH: c * DM + h * CH + CH],
                            scT[:, :], start=True, stop=True)
                        t_ia.inc(inst, 1)
                    scan["sc2"] = t_sc2.n
                    scan["ia"] = t_ia.n
                kk.seg("tensor", f_tens2)

                def f_vec2(e, en, c=c, cg=cg):
                    W(e, en, c_cg, tok["cg_all"])
                    for h in range(H):
                        W(e, en, t_sc2, cg * H + h + 1)
                        inst = e.tensor_mul(scT[:, :], psC[:, 0:128],
                                            dm_bf[:, h * CH:(h + 1) * CH])
                        v_mk.inc(inst, 1)
                        bC.release_at(v_mk, v_mk.n)
                        W(e, en, t_ia, cg * H + h + 1)
                        mg = stg[:, 512:640]
                        e.tensor_mul(mg, psD[:, 0:128], dq_bf[:, h * CH:(h + 1) * CH])
                        inst = e.tensor_add(
                            attn_sb[:, h * GT + c * CH: h * GT + c * CH + CH],
                            mg, psG[:, 0:128])
                        v_mg.inc(inst, 1)
                        bD.release_at(v_mg, v_mg.n)
                        bG.release_at(v_mg, v_mg.n)
                        W(e, en, (c_aw0, c_aw1)[cg % 2], 16 * (cg // 2 + 1))
                        src = awin if cg % 2 == 0 else gall[:, 0:2048]
                        inst = e.scalar_tensor_tensor(
                            S_sb[:, h * CH:(h + 1) * CH],
                            S_sb[:, h * CH:(h + 1) * CH], lamc[h],
                            src[:, h * CH:(h + 1) * CH], AX.mult, AX.add)
                        v_su.inc(inst, 1)
                kk.seg("vector", f_vec2)

                def f_scast(e, en, c=c, cg=cg):
                    W(e, en, v_su, NCH * H + (cg + 1) * H)
                    W(e, en, t_ia, (cg + 1) * H)
                    s_cast.inc(e.activation(Sbf[:, :], S_sb[:, :], AF.Identity), 1)
                kk.seg("scalar", f_scast)

            def f_gp2(e, en):
                guard_streams(e, en, st_wo)
                for f in range(KD):
                    sl = 2 + f % 2
                    st_wo.load(e, en, lambda f=f, sl=sl: e.dma_start(
                        out=wstr[:, 2048 * sl:2048 * sl + 2048], in_=d_wo[f]))
            kk.seg("gpsimd", f_gp2)

            sop_pre = s_op.n
            base_vr = v_rs.n
            base_top = t_op.n

            def f_tens3(e, en):
                W(e, en, v_mg, (g + 1) * NCHG * H)
                for f in range(KD):
                    ti = g * KD + f
                    st_wo.wait_tile(e, en, ti)
                    b = (bE, bF)[f % 2]
                    b.begin(e, en, rel=(s_op, sop_pre + f - 1) if f >= 2 else None)
                    sl = 2 + f % 2
                    slc = wstr[:, 2048 * sl:2048 * sl + 2048]
                    for kdi in range(KD):
                        inst = e.matmul(b.ps[:, :], slc[:, kdi * 128:(kdi + 1) * 128],
                                        attn_sb[:, kdi * GT:(kdi + 1) * GT],
                                        start=(kdi == 0), stop=(kdi == KD - 1))
                    t_op.inc(inst, 1)
            kk.seg("tensor", f_tens3)

            def f_scal3(e, en):
                for f in range(KD):
                    W(e, en, t_op, base_top + f + 1)
                    if f >= 2:
                        W(e, en, v_rs, base_vr + f - 1)
                    b = (bE, bF)[f % 2]
                    tb = tb_slots[f % 2][:, :]
                    inst = e.activation(tb, b.ps[:, :], AF.Identity,
                                        bias=bga_sb[:, f:f + 1],
                                        scale=modsb[:, 32 + f:33 + f])
                    s_op.inc(inst, 1)
                    b.release_at(s_op, s_op.n)
            kk.seg("scalar", f_scal3)

            def f_vec3(e, en):
                for f in range(KD):
                    W(e, en, s_op, sop_pre + f + 1)
                    xs = x_sb[:, f * TLOC + g * GT: f * TLOC + g * GT + GT]
                    tb = tb_slots[f % 2][:, :]
                    inst = e.tensor_add(xs, xs, tb)
                    v_rs.inc(inst, 1)
            kk.seg("vector", f_vec3)

        for g in range(NG):
            pass2_group(g)

        ln_stats()

        def ffn_group(g):
            h_pass(g, sa_base=48, ca_base=64)

            base_cs = t_cs.n
            sh_full = s_h.n

            def f_cs(e, en):
                bG.begin(e, en)
                for kdi in range(KD):
                    W(e, en, s_h, sh_full - KD + kdi + 1)
                    inst = e.matmul(psG[0:1, :], onec[:, :],
                                    h_sb[:, kdi * GT:(kdi + 1) * GT],
                                    start=(kdi == 0), stop=(kdi == KD - 1))
                t_cs.inc(inst, 1)
            kk.seg("tensor", f_cs)

            base_vcs = v_cs.n

            def f_csv(e, en):
                W(e, en, t_cs, base_cs + 1)
                inst = e.tensor_scalar_mul(ch_row, psG[0:1, :], -ZP)
                v_cs.inc(inst, 1)
                bG.release_at(v_cs, v_cs.n)
            kk.seg("vector", f_csv)

            def f_chb(e, en):
                W(e, en, v_cs, base_vcs + 1)
                bA.begin(e, en)
                inst = e.matmul(psA[:, :], ones_row[0:1, 0:128], ch_row,
                                start=True, stop=True)
                t_chb.inc(inst, 1)
            kk.seg("tensor", f_chb)

            def f_chbs(e, en):
                W(e, en, t_chb, g + 1)
                W(e, en, v_zc, g * 2 * NJ)
                inst = e.activation(chb_sb[:, :], psA[:, :], AF.Identity)
                s_chb.inc(inst, 1)
                bA.release_at(s_chb, s_chb.n)
            kk.seg("scalar", f_chbs)

            def f_gp(e, en):
                guard_streams(e, en, st_wg)
                for j in range(NJ):
                    s0 = (2 * j) % 4
                    s1 = (2 * j + 1) % 4
                    st_wg.load(e, en, lambda j=j, s0=s0: e.dma_start(
                        out=wstr[:, 2048 * s0:2048 * s0 + 2048], in_=d_wg[j]))
                    st_wg.load(e, en, lambda j=j, s1=s1: e.dma_start(
                        out=wstr[:, 2048 * s1:2048 * s1 + 2048], in_=d_wu[j]))
            kk.seg("gpsimd", f_gp)

            base_tff = t_ff.n
            base_sff = s_ff.n
            base_sfg = s_fg.n
            base_vff = v_ff.n

            base_vzc = v_zc.n

            def f_tens(e, en):
                def colsum(jj):
                    W(e, en, v_ff, base_vff + jj + 1)
                    if jj == 0:
                        bG.begin(e, en)
                    inst = e.matmul(psG[0:1, :], onec[:, :],
                                    gp_sb[:, jj * 512:(jj + 1) * 512],
                                    start=(jj == 0), stop=(jj == NJ - 1))
                    if jj == NJ - 1:
                        t_cs.inc(inst, 1)
                for j in range(NJ):
                    tg = g * 2 * NJ + 2 * j
                    st_wg.wait_tile(e, en, tg)
                    st_wg.wait_tile(e, en, tg + 1)
                    for bi, b in enumerate(((bC, bD), (bE, bF))[j % 2]):
                        relch = ((s_fg, base_sfg + j - 1) if bi == 0
                                 else (s_ff, base_sff + j - 1))
                        b.begin(e, en, rel=relch if j >= 2 else None)
                        sl = wstr[:, 2048 * ((2 * j + bi) % 4):
                                  2048 * ((2 * j + bi) % 4) + 2048]
                        for kdi in range(KD):
                            if j == 0:
                                W(e, en, s_h, sh_full - KD + kdi + 1)
                            inst = e.matmul(b.ps[:, :], sl[:, kdi * 128:(kdi + 1) * 128],
                                            h_sb[:, kdi * GT:(kdi + 1) * GT],
                                            start=(kdi == 0), stop=(kdi == KD - 1))
                        if bi == 1:
                            t_ff.inc(inst, 1)
                    if j >= 2:
                        colsum(j - 2)
                colsum(NJ - 2)
                colsum(NJ - 1)
            kk.seg("tensor", f_tens)

            def f_scal(e, en):
                W(e, en, c_k, tok["ck_all"])
                for j in range(NJ):
                    bg, bu = ((bC, bD), (bE, bF))[j % 2]
                    W(e, en, v_zc, base_vzc + 2 * j + 1)
                    inst = e.activation(gp_sb[:, j * 512:(j + 1) * 512], bg.ps[:, :],
                                        AF.Silu, scale=sg_sb[:, j:j + 1])
                    s_fg.inc(inst, 1)
                    bg.release_at(s_fg, s_fg.n)
                    if j > 0:
                        W(e, en, v_ff, base_vff + j)
                    W(e, en, v_zc, base_vzc + 2 * j + 2)
                    inst = e.activation(stg[:, 0:512], bu.ps[:, :], AF.Identity,
                                        scale=su_sb[:, j:j + 1])
                    s_ff.inc(inst, 1)
                    bu.release_at(s_ff, s_ff.n)
            kk.seg("scalar", f_scal)

            def f_vec(e, en):
                for j in range(NJ):
                    bg, bu = ((bC, bD), (bE, bF))[j % 2]
                    W(e, en, t_ff, base_tff + j + 1)
                    W(e, en, s_chb, g + 1)
                    v_zc.inc(e.tensor_add(bg.ps[:, :], bg.ps[:, :], chb_sb[:, :]), 1)
                    v_zc.inc(e.tensor_add(bu.ps[:, :], bu.ps[:, :], chb_sb[:, :]), 1)
                    W(e, en, s_ff, base_sff + j + 1)
                    W(e, en, s_fg, base_sfg + j + 1)
                    gpj = gp_sb[:, j * 512:(j + 1) * 512]
                    inst = e.tensor_mul(gpj, stg[:, 0:512], gpj)
                    v_ff.inc(inst, 1)
            kk.seg("vector", f_vec)

            base_cs2 = t_cs.n

            def f_cs2v(e, en):
                W(e, en, t_cs, base_cs2)
                inst = e.tensor_scalar_mul(cp_row, psG[0:1, :], -ZP)
                v_cs.inc(inst, 1)
                bG.release_at(v_cs, v_cs.n)
            kk.seg("vector", f_cs2v)

            def f_gp2(e, en):
                guard_streams(e, en, st_wd)
                for m in range(16):
                    for qq in range(4):
                        sl = (4 * m + qq) % 4
                        st_wd.load(e, en, lambda m=m, qq=qq, sl=sl: e.dma_start(
                            out=wstr[:, 2048 * sl:2048 * sl + 1408],
                            in_=d_wd[m][:, qq * 1408:(qq + 1) * 1408]))
            kk.seg("gpsimd", f_gp2)

            base_top = t_op.n
            base_sop = s_op.n
            base_vrs = v_rs.n
            vcs_now = v_cs.n

            def f_tens2(e, en):
                W(e, en, v_cs, vcs_now)
                for m in range(16):
                    b = (bE, bF)[m % 2]
                    b.begin(e, en, rel=(s_op, base_sop + m - 1) if m >= 2 else None)
                    e.matmul(b.ps[:, :], ones_row[0:1, 0:128], cp_row,
                             start=True, stop=False)
                    for qq in range(4):
                        ti = g * 64 + m * 4 + qq
                        st_wd.wait_tile(e, en, ti)
                        sl = wstr[:, 2048 * ((4 * m + qq) % 4):
                                  2048 * ((4 * m + qq) % 4) + 1408]
                        for hh in range(11):
                            hc = qq * 11 + hh
                            inst = e.matmul(b.ps[:, :], sl[:, hh * 128:(hh + 1) * 128],
                                            gp_sb[:, hc * 512:(hc + 1) * 512],
                                            start=False,
                                            stop=(qq == 3 and hh == 10))
                        if qq < 3:
                            t_dq3.inc(inst, 1)
                    t_op.inc(inst, 1)
            kk.seg("tensor", f_tens2)

            def f_scal2(e, en):
                for m in range(16):
                    W(e, en, t_op, base_top + m + 1)
                    if m >= 2:
                        W(e, en, v_rs, base_vrs + m - 1)
                    b = (bE, bF)[m % 2]
                    tb = tb_slots[m % 2][:, :]
                    inst = e.activation(tb, b.ps[:, :], AF.Identity,
                                        scale=sdgb_sb[:, m:m + 1])
                    s_op.inc(inst, 1)
                    b.release_at(s_op, s_op.n)
            kk.seg("scalar", f_scal2)

            def f_vec2(e, en):
                for m in range(16):
                    W(e, en, s_op, base_sop + m + 1)
                    xs = x_sb[:, m * TLOC + g * GT: m * TLOC + g * GT + GT]
                    tb = tb_slots[m % 2][:, :]
                    inst = e.tensor_add(xs, xs, tb)
                    v_rs.inc(inst, 1)
            kk.seg("vector", f_vec2)

            def f_ostream(e, en):
                for m in range(16):
                    W(e, en, v_rs, base_vrs + m + 1)
                    c_out.inc(e.dma_start(
                        out=d_out[:, m * TLOC + g * GT: m * TLOC + g * GT + GT],
                        in_=x_sb[:, m * TLOC + g * GT: m * TLOC + g * GT + GT]), 16)
            kk.seg("sync", f_ostream)

        for g in range(NG):
            ffn_group(g)

        vr_final = v_rs.n

        def f_out(e, en):
            W(e, en, v_rs, vr_final)
            if DEBUG_TAPS:
                c_out.inc(e.dma_start(out=dbg["dbg_modsb"], in_=modsb[:, :]), 16)
                c_out.inc(e.dma_start(out=dbg["dbg_rows"], in_=rows[0:1, :]), 16)
                c_out.inc(e.dma_start(out=dbg["dbg_S"], in_=S_sb[:, :]), 16)
                c_out.inc(e.dma_start(out=dbg["dbg_attn"], in_=attn_sb[:, :]), 16)
                c_out.inc(e.dma_start(out=dbg["dbg_hsp"], in_=d_hsp.ap()), 16)
                c_out.inc(e.dma_start(out=dbg["dbg_ksp"], in_=d_ksp.ap()), 16)
                c_out.inc(e.dma_start(out=dbg["dbg_vsp"], in_=d_vsp.ap()), 16)
                c_out.inc(e.dma_start(out=dbg["dbg_asp"], in_=d_asp.ap()), 16)
                c_out.inc(e.dma_start(out=dbg["dbg_gco"], in_=d_gco.ap()), 16)
            W(e, en, c_out, c_out.n)
        kk.seg("sync", f_out)

    bad = [(en, ch.sem.name, v, ch.n) for (en, ch, v) in kk.audit if v > ch.n]
    assert not bad, f"waits exceeding final counts: {bad[:10]}"
    return nc




def kernel(L0_attn_x, L0_attn_w_qkv, L0_attn_b_qkv, L0_attn_w_out, L0_attn_b_out,
           L0_ffn_wq_gate, L0_ffn_s_gate, L0_ffn_wq_up, L0_ffn_s_up,
           L0_ffn_wq_down, L0_ffn_s_down, L0_c, L0_wmod):
    x = np.asarray(L0_attn_x, dtype=np.float32)
    w_qkv = np.asarray(L0_attn_w_qkv, dtype=np.float32)
    b_qkv = np.asarray(L0_attn_b_qkv, dtype=np.float32)
    w_out = np.asarray(L0_attn_w_out, dtype=np.float32)
    b_out = np.asarray(L0_attn_b_out, dtype=np.float32)
    s_g = np.asarray(L0_ffn_s_gate, dtype=np.float32)
    s_u = np.asarray(L0_ffn_s_up, dtype=np.float32)
    s_d = np.asarray(L0_ffn_s_down, dtype=np.float32)
    cvec = np.asarray(L0_c, dtype=np.float32)
    wmod = np.asarray(L0_wmod, dtype=np.float32)

    def u8(a):
        a = np.ascontiguousarray(np.asarray(a, dtype=np.int32))
        return np.ascontiguousarray(a.view(np.uint8).reshape(a.shape + (4,))[..., 0])

    wg8 = u8(L0_ffn_wq_gate)
    wu8_ = u8(L0_ffn_wq_up)
    wd8 = u8(L0_ffn_wq_down)

    dm32, dq32, dk32, _, lamc = _host_consts(0)
    ident = np.eye(128, dtype=np.float32)

    wq_blob = _lhsT_blob(np.ascontiguousarray(w_qkv[0:2048].T), 16)
    wk_blob = _lhsT_blob(np.ascontiguousarray(w_qkv[2048:4096].T), 16)
    wv_blob = _wv_blob(np.ascontiguousarray(w_qkv[4096:6144].T))
    wo_blob = _lhsT_blob(np.ascontiguousarray(w_out.T), 16)
    wg_blob = _wg_blob(np.ascontiguousarray(wg8.T))
    wu_blob = _wg_blob(np.ascontiguousarray(wu8_.T))
    wd_blob = _wd_blob(np.ascontiguousarray(wd8.T))

    bq_c = _col16(b_qkv[0:2048])
    bk_c = _col16(b_qkv[2048:4096])
    bv_r = np.ascontiguousarray(b_qkv[4096:6144]).reshape(1, DM)
    bo_c = _col16(b_out)
    sg_c = _col44(s_g)
    su_c = _col44(s_u)
    sd_c = _col16(s_d)
    cc_c = _col16(cvec[0])

    nc = build_program(lamc)

    in_maps = []
    for core in range(NC):
        _, _, _, wc32, _ = _host_consts(core)
        in_maps.append({
            "x": _x_blob(x, core),
            "dmask": dm32, "dqt": dq32, "ident": ident, "dk": dk32,
            "wcomb": wc32, "cvec": cc_c,
            "bq": bq_c, "bk": bk_c, "bv": bv_r, "bo": bo_c,
            "sg": sg_c, "su": su_c, "sd": sd_c,
            "wq": wq_blob, "wk": wk_blob, "wv": wv_blob, "wo": wo_blob,
            "wg": wg_blob, "wu": wu_blob, "wd": wd_blob,
            "wm": _wm_blob(wmod, core),
        })

    res = run_bass_kernel_spmd(nc, in_maps, core_ids=list(range(NC)))

    out = np.empty((1, SEQ, DM), dtype=np.float32)
    for core in range(NC):
        blob = res.results[core]["out"]
        out[0, core * TLOC:(core + 1) * TLOC, :] = (
            np.asarray(blob).reshape(128, KD, TLOC).transpose(2, 1, 0)
            .reshape(TLOC, DM))
    return out


# version 16 (final)
# speedup vs baseline: 1.0053x; 1.0053x over previous
import math
from contextlib import ExitStack

import numpy as np

import concourse.bass as bass
import concourse.mybir as mybir
from concourse.bass_utils import run_bass_kernel_spmd

F32 = mybir.dt.float32
BF16 = mybir.dt.bfloat16
U8 = mybir.dt.uint8
AX = mybir.AluOpType
AF = mybir.ActivationFunctionType

NC = 8
H = 16
DH = 128
DM = 2048
KD = DM // 128
SEQ = 8192
TLOC = SEQ // NC
NG = 2
GT = 512
CH = 128
NCHG = GT // CH
NCH = TLOC // CH
HID = 5632
NJ = HID // 128
MODLOC = 12288 // NC
EPS = 1e-5
ZP = 128.0


def _decays():
    start = 2.0 ** (-(2.0 ** (-(math.log2(H) - 3))))
    slopes = np.array([start ** (i + 1) for i in range(H)], dtype=np.float32)
    return np.exp(-slopes.astype(np.float64))


def _host_consts(core):
    lam = _decays()
    pos = np.arange(CH, dtype=np.float64)
    dm = np.zeros((CH, H * CH), dtype=np.float64)
    dq = np.zeros((CH, H * CH), dtype=np.float64)
    dk = np.zeros((CH, H), dtype=np.float64)
    wc = np.zeros((CH, 128), dtype=np.float64)
    for h in range(H):
        M = lam[h] ** np.maximum(pos[None, :] - pos[:, None], 0.0)
        dm[:, h * CH:(h + 1) * CH] = np.where(pos[None, :] >= pos[:, None], M, 0.0)
        dq[:, h * CH:(h + 1) * CH] = (lam[h] ** (pos + 1.0))[None, :]
        dk[:, h] = lam[h] ** (CH - 1.0 - pos)
        for j in range(core):
            wc[:, j * H + h] = lam[h] ** (1024.0 * (core - 1 - j))
    lamc = [float(lam[h] ** CH) for h in range(H)]
    return (dm.astype(np.float32), dq.astype(np.float32),
            dk.astype(np.float32), wc.astype(np.float32), lamc)



def _x_blob(x, core):
    sl = np.asarray(x[0, core * TLOC:(core + 1) * TLOC, :])
    return np.ascontiguousarray(sl.T.reshape(KD, 128, TLOC).transpose(1, 0, 2)
                                .reshape(128, KD * TLOC))


def _lhsT_blob(WT, nf):
    a = WT.reshape(KD, 128, nf, 128).transpose(2, 1, 0, 3)
    return np.ascontiguousarray(a.reshape(nf, 128, KD * 128))


def _wv_blob(WvT):
    a = WvT.reshape(4, 4, 128, 4, 512).transpose(3, 0, 2, 1, 4)
    return np.ascontiguousarray(a.reshape(4, 4, 128, 4 * 512))


def _wd_blob(WdT):
    a = WdT.reshape(NJ, 128, 16, 128).transpose(2, 1, 0, 3)
    return np.ascontiguousarray(a.reshape(16, 128, NJ * 128))


def _wg_blob(WT):
    a = WT.reshape(KD, 128, NJ, 128).transpose(2, 1, 0, 3)
    return np.ascontiguousarray(a.reshape(NJ, 128, KD * 128))


def _wm_blob(wmod, core):
    lo = core * MODLOC
    a = np.asarray(wmod[:, lo:lo + MODLOC]).reshape(KD, 128, 12, 128)
    return np.ascontiguousarray(a.transpose(2, 1, 0, 3).reshape(12, 128, KD * 128))


def _col16(v):
    return np.ascontiguousarray(np.asarray(v).reshape(16, 128).T)


def _col44(v):
    return np.ascontiguousarray(np.asarray(v).reshape(NJ, 128).T)



class Ch:
    def __init__(self, stack, nc, name):
        self.sem = stack.enter_context(nc.semaphore(name))
        self.n = 0

    def inc(self, inst, k=1):
        inst.then_inc(self.sem, k)
        self.n += k
        return self.n


class K:
    def __init__(self, nc, stack):
        self.nc = nc
        self.stack = stack
        self.block = stack.enter_context(nc.Block())
        self.chans = {}
        self.waited = {}
        self.audit = []

    def ch(self, name):
        if name not in self.chans:
            self.chans[name] = Ch(self.stack, self.nc, name)
        return self.chans[name]

    def wait(self, eng, engname, ch, val):
        if val <= 0:
            return
        key = (engname, ch.sem.name)
        if self.waited.get(key, 0) >= val:
            return
        self.audit.append((engname, ch, val))
        eng.wait_ge(ch.sem, val)
        self.waited[key] = val

    def seg(self, engname, fn):
        def wrap(eng):
            fn(eng, engname)
        getattr(self.block, engname)(wrap)


class Bank:

    def __init__(self, K_, ps, name):
        self.K = K_
        self.ps = ps
        self.marks = []

    def begin(self, eng, engname, rel=None):
        if self.marks:
            ch, val = self.marks[-1]
            self.K.wait(eng, engname, ch, val)
        if rel is not None:
            ch, val = rel
            self.K.wait(eng, engname, ch, val)

    def release_at(self, ch, val):
        self.marks.append((ch, val))


class Stream:

    def __init__(self, K_, name, nslots):
        self.K = K_
        self.lds = [K_.ch(f"{name}_ld{i}") for i in range(nslots)]
        self.nslots = nslots
        self.iprod = 0
        self.mark_of = None

    def load(self, eng, engname, dma_fn):
        i = self.iprod
        if i >= self.nslots:
            ch, val = self.mark_of(i - self.nslots)
            self.K.wait(eng, engname, ch, val)
        self.lds[i % self.nslots].inc(dma_fn(), 16)
        self.iprod += 1

    def wait_tile(self, eng, engname, i):
        self.K.wait(eng, engname, self.lds[i % self.nslots],
                    16 * (i // self.nslots + 1))


DEBUG_TAPS = False


def build_program(lamc):
    nc = bass.Bass("TRN2", target_bir_lowering=False,
                   detect_race_conditions=False)

    d_x = nc.dram_tensor("x", [128, KD * TLOC], F32, kind="ExternalInput").ap()
    d_out = nc.dram_tensor("out", [128, KD * TLOC], F32, kind="ExternalOutput").ap()
    d_dm = nc.dram_tensor("dmask", [128, H * CH], F32, kind="ExternalInput").ap()
    d_dq = nc.dram_tensor("dqt", [128, H * CH], F32, kind="ExternalInput").ap()
    d_id = nc.dram_tensor("ident", [128, 128], F32, kind="ExternalInput").ap()
    d_dk = nc.dram_tensor("dk", [128, H], F32, kind="ExternalInput").ap()
    d_wc = nc.dram_tensor("wcomb", [128, 128], F32, kind="ExternalInput").ap()
    d_cc = nc.dram_tensor("cvec", [128, 16], F32, kind="ExternalInput").ap()
    d_bq = nc.dram_tensor("bq", [128, 16], F32, kind="ExternalInput").ap()
    d_bk = nc.dram_tensor("bk", [128, 16], F32, kind="ExternalInput").ap()
    d_bv = nc.dram_tensor("bv", [1, DM], F32, kind="ExternalInput").ap()
    d_bo = nc.dram_tensor("bo", [128, 16], F32, kind="ExternalInput").ap()
    d_sg = nc.dram_tensor("sg", [128, NJ], F32, kind="ExternalInput").ap()
    d_su = nc.dram_tensor("su", [128, NJ], F32, kind="ExternalInput").ap()
    d_sd = nc.dram_tensor("sd", [128, 16], F32, kind="ExternalInput").ap()
    d_wq = nc.dram_tensor("wq", [16, 128, KD * 128], F32, kind="ExternalInput").ap()
    d_wk = nc.dram_tensor("wk", [16, 128, KD * 128], F32, kind="ExternalInput").ap()
    d_wv = nc.dram_tensor("wv", [4, 4, 128, 4 * 512], F32, kind="ExternalInput").ap()
    d_wo = nc.dram_tensor("wo", [16, 128, KD * 128], F32, kind="ExternalInput").ap()
    d_wg = nc.dram_tensor("wg", [NJ, 128, KD * 128], U8, kind="ExternalInput").ap()
    d_wu = nc.dram_tensor("wu", [NJ, 128, KD * 128], U8, kind="ExternalInput").ap()
    d_wd = nc.dram_tensor("wd", [16, 128, NJ * 128], U8, kind="ExternalInput").ap()
    d_wm = nc.dram_tensor("wm", [12, 128, KD * 128], F32, kind="ExternalInput").ap()

    d_mci = nc.dram_tensor("mod_ci", [128, 12], F32)
    d_mco = nc.dram_tensor("mod_co", [NC, 128, 12], F32)
    d_gci = nc.dram_tensor("g_ci", [128, H * CH], BF16)
    d_gco = nc.dram_tensor("g_co", [NC, 128, H * CH], BF16)
    if DEBUG_TAPS:
        dbg = {
            "dbg_modsb": nc.dram_tensor("dbg_modsb", [128, 96], F32, kind="ExternalOutput").ap(),
            "dbg_rows": nc.dram_tensor("dbg_rows", [1, 2048], BF16, kind="ExternalOutput").ap(),
            "dbg_S": nc.dram_tensor("dbg_S", [128, H * CH], F32, kind="ExternalOutput").ap(),
            "dbg_attn": nc.dram_tensor("dbg_attn", [128, H * GT], BF16, kind="ExternalOutput").ap(),
            "dbg_hsp": nc.dram_tensor("dbg_hsp", [128, NG * KD * GT], BF16, kind="ExternalOutput").ap(),
            "dbg_ksp": nc.dram_tensor("dbg_ksp", [128, NG * H * GT], BF16, kind="ExternalOutput").ap(),
            "dbg_vsp": nc.dram_tensor("dbg_vsp", [128, NG * NCHG * DM], BF16, kind="ExternalOutput").ap(),
            "dbg_asp": nc.dram_tensor("dbg_asp", [128, NCH * H * CH], BF16, kind="ExternalOutput").ap(),
            "dbg_gco": nc.dram_tensor("dbg_gco", [NC, 128, H * CH], BF16, kind="ExternalOutput").ap(),
        }
    d_hsp = nc.dram_tensor("h_sp", [128, NG * KD * GT], BF16)
    d_ksp = nc.dram_tensor("k_sp", [128, NG * H * GT], BF16)
    d_vsp = nc.dram_tensor("v_sp", [128, NG * NCHG * DM], BF16)
    d_asp = nc.dram_tensor("a_sp", [128, NCH * H * CH], BF16)

    stack = ExitStack()
    with stack:
        sbt = lambda name, shape, dt: stack.enter_context(nc.sbuf_tensor(name, shape, dt))
        pst = lambda name, shape, dt: stack.enter_context(nc.psum_tensor(name, shape, dt))

        x_sb = sbt("x_sb", [128, KD * TLOC], F32)
        arena = sbt("arena", [128, 24576], BF16)
        q_sb = arena[:, 0:8192]
        k_sb = arena[:, 8192:16384]
        v_sb = arena[:, 16384:24576]
        gp_sb = arena[:, 0:NJ * 512]
        h_sb = sbt("h_sb", [128, KD * GT], BF16)
        attn_sb = sbt("attn_sb", [128, H * GT], BF16)
        wstr = sbt("wstr", [128, 4 * 2048], BF16)
        S_sb = sbt("S_sb", [128, H * CH], F32)
        Sbf = sbt("Sbf", [128, H * CH], BF16)
        dm_bf = sbt("dm_bf", [128, H * CH], BF16)
        dq_bf = sbt("dq_bf", [128, H * CH], BF16)
        id_bf = sbt("id_bf", [128, 128], BF16)
        stg = sbt("stg", [128, 1024], F32)
        t3bf = sbt("t3bf", [128, 512], BF16)
        awin = sbt("awin", [128, H * CH], BF16)
        gall = sbt("gall", [128, 2 * NC * CH], BF16)
        xb_sb = sbt("xb_sb", [128, 512], BF16)
        x2b_sb = sbt("x2b_sb", [128, 512], BF16)
        tb_slots = (t3bf, x2b_sb)
        chb_sb = xb_sb
        rows = sbt("rows", [1, 2048], BF16)
        brow = sbt("brow", [1, DM + 512], BF16)
        scT = sbt("scT", [128, 128], BF16)
        ktil = sbt("ktil", [128, 128], BF16)
        dk_sb = sbt("dk_sb", [128, H], F32)
        wc_sb = sbt("wc_sb", [128, 128], F32)
        cc_sb = sbt("cc_sb", [128, 16], F32)
        siluc = sbt("siluc", [128, 16], F32)
        bq_sb = sbt("bq_sb", [128, 16], F32)
        bk_sb = sbt("bk_sb", [128, 16], F32)
        bo_sb = sbt("bo_sb", [128, 16], F32)
        sg_sb = sbt("sg_sb", [128, NJ], F32)
        su_sb = sbt("su_sb", [128, NJ], F32)
        sd_sb = sbt("sd_sb", [128, 16], F32)
        modsb = sbt("modsb", [128, 96], F32)
        modpart = sbt("modpart", [128, 12], F32)
        bga_sb = sbt("bga_sb", [128, 16], F32)
        sdgb_sb = sbt("sdgb_sb", [128, 16], F32)
        tmpa = sbt("tmpa", [128, 16], F32)
        tmpb = sbt("tmpb", [128, 16], F32)
        onec = sbt("onec", [128, 1], BF16)
        ones_row = brow[0:1, DM:DM + 512]
        bv_row = brow[0:1, 0:DM]
        ch_row = brow[0:1, 0:512]
        cp_row = brow[0:1, 512:1024]

        psA = pst("psA", [128, 512], F32)
        psB = pst("psB", [128, 512], F32)
        psC = pst("psC", [128, 512], F32)
        psD = pst("psD", [128, 512], F32)
        psE = pst("psE", [128, 512], F32)
        psF = pst("psF", [128, 512], F32)
        psG = pst("psG", [128, 512], F32)
        psH = pst("psH", [128, 1024], BF16)

        kk = K(nc, stack)
        W = kk.wait
        bA = Bank(kk, psA, "bA"); bB = Bank(kk, psB, "bB")
        bC = Bank(kk, psC, "bC"); bD = Bank(kk, psD, "bD")
        bE = Bank(kk, psE, "bE"); bF = Bank(kk, psF, "bF")
        bG = Bank(kk, psG, "bG"); bH = Bank(kk, psH, "bH")

        st_wm = Stream(kk, "wm", 2)
        st_wk = Stream(kk, "swk", 2)
        st_wq = Stream(kk, "swq", 2)
        st_wv = Stream(kk, "swv", 2)
        st_wo = Stream(kk, "swo", 2)
        st_wg = Stream(kk, "swg", 4)
        st_wd = Stream(kk, "swd", 4)
        all_streams = [st_wm, st_wk, st_wq, st_wv, st_wo, st_wg, st_wd]

        def guard_streams(e, en, this):
            for s in all_streams:
                if s is this or s.iprod == 0:
                    continue
                best = {}
                for i in range(s.iprod):
                    ch, val = s.mark_of(i)
                    if val > best.get(id(ch), (ch, 0))[1]:
                        best[id(ch)] = (ch, val)
                for ch, val in best.values():
                    W(e, en, ch, val)

        c_x = kk.ch("c_x")
        c_x2 = kk.ch("c_x2")
        c_k = kk.ch("c_k")
        v_zc = kk.ch("v_zc")
        t_chb = kk.ch("t_chb")
        s_chb = kk.ch("s_chb")
        c_cg = kk.ch("c_cg")
        c_cc = kk.ch("c_cc")
        c_ms = kk.ch("c_ms")
        v_ms = kk.ch("v_ms")
        s_sc = kk.ch("s_sc")
        t_mod = kk.ch("t_mod")
        s_mev = kk.ch("s_mev")
        v_mod = kk.ch("v_mod")
        t_st = kk.ch("t_st")
        s_xb = kk.ch("s_xb")
        v_x2 = kk.ch("v_x2")
        v_sev = kk.ch("v_sev")
        s_sq = kk.ch("s_sq")
        v_rc = kk.ch("v_rc")
        t_mr = kk.ch("t_mr")
        v_h1 = kk.ch("v_h1")
        s_h = kk.ch("s_h")
        t_kp = kk.ch("t_kp")
        s_kp = kk.ch("s_kp")
        t_vk = kk.ch("t_vk")
        s_vp = kk.ch("s_vp")
        t_qp = kk.ch("t_qp")
        s_qp = kk.ch("s_qp")
        t_kt = kk.ch("t_kt")
        s_kt = kk.ch("s_kt")
        t_am = kk.ch("t_am")
        v_su = kk.ch("v_su")
        s_ae = kk.ch("s_ae")
        c_sph = kk.ch("c_sph")
        c_spk = kk.ch("c_spk")
        c_spv = kk.ch("c_spv")
        c_spa = kk.ch("c_spa")
        c_gi = kk.ch("c_gi")
        c_l2h = kk.ch("c_l2h")
        c_l2k = kk.ch("c_l2k")
        c_l2v = kk.ch("c_l2v")
        s_gc = kk.ch("s_gc")
        c_gw0 = kk.ch("c_gw0")
        c_gw1 = kk.ch("c_gw1")
        v_cb = kk.ch("v_cb")
        c_aw0 = kk.ch("c_aw0")
        c_aw1 = kk.ch("c_aw1")
        t_sc2 = kk.ch("t_sc2")
        v_mk = kk.ch("v_mk")
        t_ia = kk.ch("t_ia")
        v_mg = kk.ch("v_mg")
        s_cast = kk.ch("s_cast")
        t_op = kk.ch("t_op")
        t_dq3 = kk.ch("t_dq3")
        s_op = kk.ch("s_op")
        v_rs = kk.ch("v_rs")
        t_ff = kk.ch("t_ff")
        s_fg = kk.ch("s_fg")
        s_ff = kk.ch("s_ff")
        v_ff = kk.ch("v_ff")
        t_cs = kk.ch("t_cs")
        v_cs = kk.ch("v_cs")
        c_out = kk.ch("c_out")

        tok = {}

        st_wm.mark_of = lambda i: (t_mod, i + 1)
        st_wk.mark_of = lambda i: (t_kp, i + 1)
        st_wq.mark_of = lambda i: (t_qp, i + 1)
        st_wv.mark_of = lambda i: (t_vk, i + 1)
        st_wo.mark_of = lambda i: (t_op, i + 1)
        st_wg.mark_of = lambda i: (t_ff, i // 2 + 1)
        st_wd.mark_of = lambda i: (
            (t_op, 32 + (i // 64) * 16 + (i % 64) // 4 + 1)
            if i % 4 == 3 else
            (t_dq3, (i // 64) * 48 + ((i % 64) // 4) * 3 + (i % 4) + 1))

        def p0_sync(e, en):
            for dst, src in ((cc_sb, d_cc), (dk_sb, d_dk), (wc_sb, d_wc),
                             (bq_sb, d_bq), (bk_sb, d_bk), (bo_sb, d_bo),
                             (sg_sb, d_sg), (su_sb, d_su), (sd_sb, d_sd)):
                c_k.inc(e.dma_start(out=dst[:, :], in_=src), 16)
            tok["ck_all"] = c_k.n
            x3 = x_sb[:, :].rearrange("p (f t) -> p f t", f=KD)
            d3 = d_x.rearrange("p (f t) -> p f t", f=KD)
            c_x.inc(e.dma_start(out=x3[:, :, 0:GT], in_=d3[:, :, 0:GT]), 16)
            c_x2.inc(e.dma_start(out=x3[:, :, GT:TLOC], in_=d3[:, :, GT:TLOC]), 16)
            tok["cx_all"] = c_x.n
        kk.seg("sync", p0_sync)

        def p0_gp(e, en):
            wm32 = arena[:, 0:8192].bitcast(F32)
            for fc in range(2):
                st_wm.load(e, en, lambda fc=fc: e.dma_start(
                    out=wm32[:, (fc % 2) * 2048:(fc % 2) * 2048 + 2048],
                    in_=d_wm[fc]))
            c_cg.inc(e.dma_start(out=dm_bf[:, :], in_=d_dm), 16)
            c_cg.inc(e.dma_start(out=dq_bf[:, :], in_=d_dq), 16)
            c_cg.inc(e.dma_start(out=id_bf[:, :], in_=d_id), 16)
            c_cg.inc(e.dma_start(out=bv_row, in_=d_bv), 16)
            tok["cg_all"] = c_cg.n
            for fc in range(2, 12):
                st_wm.load(e, en, lambda fc=fc: e.dma_start(
                    out=wm32[:, (fc % 2) * 2048:(fc % 2) * 2048 + 2048],
                    in_=d_wm[fc]))
        kk.seg("gpsimd", p0_gp)

        def p0_vec(e, en):
            e.memset(S_sb[:, :], 0.0)
            e.memset(onec[:, :], 1.0)
            v_ms.inc(e.memset(ones_row, 1.0), 1)
        kk.seg("vector", p0_vec)

        def p0_scal(e, en):
            W(e, en, c_k, tok["ck_all"])
            s_sc.inc(e.activation(siluc[:, :], cc_sb[:, :], AF.Silu), 1)
        kk.seg("scalar", p0_scal)

        def p1_tensor(e, en):
            W(e, en, s_sc, 1)
            wm32 = arena[:, 0:8192].bitcast(F32)
            for fc in range(12):
                st_wm.wait_tile(e, en, fc)
                b = (bA, bB)[fc % 2]
                b.begin(e, en, rel=(s_mev, fc - 1) if fc >= 2 else None)
                sl = wm32[:, (fc % 2) * 2048:(fc % 2) * 2048 + 2048]
                for kdi in range(KD):
                    inst = e.matmul(b.ps[:, 0:1], sl[:, kdi * 128:(kdi + 1) * 128],
                                    siluc[:, kdi:kdi + 1],
                                    start=(kdi == 0), stop=(kdi == KD - 1))
                t_mod.inc(inst, 1)
        kk.seg("tensor", p1_tensor)

        def p1_scal(e, en):
            for fc in range(12):
                W(e, en, t_mod, fc + 1)
                b = (bA, bB)[fc % 2]
                inst = e.activation(modpart[:, fc:fc + 1], b.ps[:, 0:1], AF.Identity)
                s_mev.inc(inst, 1)
                b.release_at(s_mev, s_mev.n)
        kk.seg("scalar", p1_scal)

        def p1_sync(e, en):
            W(e, en, s_mev, 12)
            c_ms.inc(e.dma_start(out=d_mci.ap(), in_=modpart[:, :]), 16)
        kk.seg("sync", p1_sync)

        def p1_gp(e, en):
            W(e, en, c_ms, 16)
            inst = e.collective_compute(
                "AllGather", AX.bypass, replica_groups=[list(range(NC))],
                ins=[d_mci.ap().opt()], outs=[d_mco.ap().opt()])
            c_cc.inc(inst, 1)
        kk.seg("gpsimd", p1_gp)

        def p1b_sync(e, en):
            W(e, en, c_cc, 1)
            c_ms.inc(e.dma_start(
                out=modsb[:, :].rearrange("p (r c) -> p r c", r=NC),
                in_=d_mco.ap().transpose([1, 0, 2])), 16)
        kk.seg("sync", p1b_sync)

        def p1_vec(e, en):
            vb = kk.ch("v_bar")

            def bar(inst):
                vb.inc(inst, 1)
                e.wait_ge(vb.sem, vb.n)
            W(e, en, c_ms, 32)
            e.tensor_scalar_add(modsb[:, 16:32], modsb[:, 16:32], 1.0)
            e.tensor_scalar_add(modsb[:, 64:80], modsb[:, 64:80], 1.0)
            for base in (32, 80):
                g0 = modsb[:, base:base + 16]
                bar(e.tensor_scalar_mul(tmpa[:, :], g0, -1.0))
                bar(e.tensor_max(tmpb[:, :], g0, tmpa[:, :]))
                bar(e.tensor_scalar_max(tmpb[:, :], tmpb[:, :], 0.25))
                bar(e.tensor_scalar(tmpa[:, :], g0, 0.0, None, AX.is_ge))
                bar(e.tensor_scalar_mul(tmpa[:, :], tmpa[:, :], 2.0))
                bar(e.tensor_scalar_add(tmpa[:, :], tmpa[:, :], -1.0))
                bar(e.tensor_mul(g0, tmpb[:, :], tmpa[:, :]))
            W(e, en, c_k, tok["ck_all"])
            e.tensor_mul(bga_sb[:, :], modsb[:, 32:48], bo_sb[:, :])
            inst = e.tensor_mul(sdgb_sb[:, :], modsb[:, 80:96], sd_sb[:, :])
            v_mod.inc(inst, 1)
        kk.seg("vector", p1_vec)

        def ln_stats():
            vrs_pre = v_rs.n
            base_sev0 = v_sev.n
            for g in range(NG):
                base_t = t_st.n
                base_sx = s_xb.n
                base_v2 = v_x2.n
                base_sev = v_sev.n
                base_sq = s_sq.n

                def f_scal(e, en, g=g, base_t=base_t, base_sx=base_sx):
                    W(e, en, (c_x, c_x2)[g], 16)
                    W(e, en, v_rs, vrs_pre)
                    for f in range(KD):
                        if g * KD + f > 0:
                            W(e, en, t_st, base_t + f)
                        xs = x_sb[:, f * TLOC + g * GT: f * TLOC + g * GT + GT]
                        s_xb.inc(e.activation(xb_sb[:, :], xs, AF.Identity), 1)
                kk.seg("scalar", f_scal)

                def f_vec(e, en, g=g, base_t=base_t, base_sx=base_sx):
                    for f in range(KD):
                        W(e, en, s_xb, base_sx + f + 1)
                        if g * KD + f > 0:
                            W(e, en, t_st, base_t + f)
                        v_x2.inc(e.tensor_mul(x2b_sb[:, :], xb_sb[:, :], xb_sb[:, :]), 1)
                kk.seg("vector", f_vec)

                def f_tens(e, en, g=g, base_t=base_t, base_sx=base_sx,
                           base_v2=base_v2, base_sev0=base_sev0):
                    W(e, en, v_ms, 1)
                    bA.begin(e, en, rel=(v_sev, base_sev0 + 1) if g else None)
                    bB.begin(e, en, rel=(v_sev, base_sev0 + 2) if g else None)
                    for f in range(KD):
                        W(e, en, s_xb, base_sx + f + 1)
                        e.matmul(psA[0:1, :], onec[:, :], xb_sb[:, :],
                                 start=(f == 0), stop=(f == KD - 1))
                        W(e, en, v_x2, base_v2 + f + 1)
                        inst = e.matmul(psB[0:1, :], onec[:, :], x2b_sb[:, :],
                                        start=(f == 0), stop=(f == KD - 1))
                        t_st.inc(inst, 1)
                kk.seg("tensor", f_tens)

                def f_vec2(e, en, g=g, base_t=base_t):
                    vb = kk.ch("v_bar")
                    W(e, en, t_st, base_t + KD)
                    m_r = rows[0:1, g * GT: g * GT + GT]
                    ssq_f = stg[0:1, 0:512]
                    msq_f = stg[0:1, 512:1024]
                    i1 = e.tensor_scalar_mul(m_r, psA[0:1, :], 1.0 / DM)
                    v_sev.inc(i1, 1)
                    bA.release_at(v_sev, v_sev.n)
                    i2 = e.tensor_scalar_mul(ssq_f, psB[0:1, :], 1.0 / DM)
                    v_sev.inc(i2, 1)
                    bB.release_at(v_sev, v_sev.n)
                    e.wait_ge(v_sev.sem, v_sev.n)
                    vb.inc(e.tensor_mul(msq_f, m_r, m_r), 1)
                    e.wait_ge(vb.sem, vb.n)
                    vb.inc(e.tensor_sub(ssq_f, ssq_f, msq_f), 1)
                    e.wait_ge(vb.sem, vb.n)
                    inst = e.tensor_scalar_add(ssq_f, ssq_f, EPS)
                    v_sev.inc(inst, 1)
                kk.seg("vector", f_vec2)

                def f_sq(e, en, g=g, base_sev=base_sev):
                    W(e, en, v_sev, base_sev + 3)
                    s_sq.inc(e.activation(stg[0:1, 0:512], stg[0:1, 0:512], AF.Sqrt), 1)
                kk.seg("scalar", f_sq)

                def f_rc(e, en, g=g, base_sq=base_sq):
                    vb = kk.ch("v_bar")
                    W(e, en, s_sq, base_sq + 1)
                    vb.inc(e.reciprocal(stg[0:1, 0:512], stg[0:1, 0:512]), 1)
                    e.wait_ge(vb.sem, vb.n)
                    r_r = rows[0:1, 1024 + g * GT: 1024 + g * GT + GT]
                    v_rc.inc(e.tensor_copy(r_r, stg[0:1, 0:512]), 1)
                kk.seg("vector", f_rc)

        def h_pass(g, sa_base, ca_base):
            vrc_pre = v_rc.n
            csp_pre = c_sph.n
            h_guards = [(t_vk, t_vk.n), (t_kp, t_kp.n), (t_qp, t_qp.n),
                        (t_ff, t_ff.n), (t_cs, t_cs.n)]

            def f_tens(e, en):
                W(e, en, v_rc, vrc_pre)
                W(e, en, v_ms, 1)
                bA.begin(e, en)
                e.matmul(psA[:, :], ones_row[0:1, 0:128],
                         rows[0:1, g * GT: g * GT + GT], start=True, stop=True)
                bB.begin(e, en)
                inst = e.matmul(psB[:, :], ones_row[0:1, 0:128],
                                rows[0:1, 1024 + g * GT: 1024 + g * GT + GT],
                                start=True, stop=True)
                t_mr.inc(inst, 1)
            kk.seg("tensor", f_tens)

            base_vh = v_h1.n
            base_sh = s_h.n
            tmr = t_mr.n

            def f_vec(e, en):
                W(e, en, t_mr, tmr)
                for f in range(KD):
                    if f > 0:
                        W(e, en, s_h, base_sh + f)
                    xs = x_sb[:, f * TLOC + g * GT: f * TLOC + g * GT + GT]
                    t1 = stg[:, 0:512]
                    e.scalar_tensor_tensor(t1, xs, 0.0, psA[:, :], AX.add, AX.subtract)
                    inst = e.tensor_mul(t1, t1, psB[:, :])
                    v_h1.inc(inst, 1)
                    if f == KD - 1:
                        bA.release_at(v_h1, v_h1.n)
                        bB.release_at(v_h1, v_h1.n)
            kk.seg("vector", f_vec)

            def f_scal(e, en):
                W(e, en, v_mod, 1)
                W(e, en, c_sph, csp_pre)
                for chn, val in h_guards:
                    W(e, en, chn, val)
                for f in range(KD):
                    W(e, en, v_h1, base_vh + f + 1)
                    inst = e.activation(h_sb[:, f * GT:(f + 1) * GT], stg[:, 0:512],
                                        AF.Identity,
                                        bias=modsb[:, sa_base + f: sa_base + f + 1],
                                        scale=modsb[:, ca_base + f: ca_base + f + 1])
                    s_h.inc(inst, 1)
            kk.seg("scalar", f_scal)

        def kv_group(g):
            def f_gp(e, en):
                guard_streams(e, en, st_wk)
                for f in range(KD):
                    st_wk.load(e, en, lambda f=f: e.dma_start(
                        out=wstr[:, (f % 2) * 2048:(f % 2) * 2048 + 2048],
                        in_=d_wk[f]))
            kk.seg("gpsimd", f_gp)

            base_skp = s_kp.n
            sh_full = s_h.n
            csp_k = c_spk.n

            def f_tens(e, en):
                for f in range(KD):
                    ti = g * KD + f
                    st_wk.wait_tile(e, en, ti)
                    b = (bE, bF)[f % 2]
                    b.begin(e, en, rel=(s_kp, base_skp + f - 1) if f >= 2 else None)
                    sl = wstr[:, (f % 2) * 2048:(f % 2) * 2048 + 2048]
                    for kdi in range(KD):
                        if f == 0:
                            W(e, en, s_h, sh_full - KD + kdi + 1)
                        inst = e.matmul(b.ps[:, :], sl[:, kdi * 128:(kdi + 1) * 128],
                                        h_sb[:, kdi * GT:(kdi + 1) * GT],
                                        start=(kdi == 0), stop=(kdi == KD - 1))
                    t_kp.inc(inst, 1)
            kk.seg("tensor", f_tens)

            def f_scal(e, en):
                W(e, en, c_k, tok["ck_all"])
                W(e, en, c_spk, csp_k)
                for f in range(KD):
                    W(e, en, t_kp, g * KD + f + 1)
                    b = (bE, bF)[f % 2]
                    inst = e.activation(k_sb[:, f * GT:(f + 1) * GT], b.ps[:, :],
                                        AF.Identity, bias=bk_sb[:, f:f + 1])
                    s_kp.inc(inst, 1)
                    b.release_at(s_kp, s_kp.n)
            kk.seg("scalar", f_scal)

            def f_sync(e, en):
                W(e, en, s_kp, base_skp + KD)
                c_spk.inc(e.dma_start(out=d_ksp.ap()[:, g * 8192:(g + 1) * 8192],
                                      in_=k_sb[:, :]), 16)
            kk.seg("sync", f_sync)

            def f_gp2(e, en):
                guard_streams(e, en, st_wv)
                for vg in range(4):
                    for kg in range(4):
                        sl = 2 + (vg * 4 + kg) % 2
                        st_wv.load(e, en, lambda vg=vg, kg=kg, sl=sl: e.dma_start(
                            out=wstr[:, 2048 * sl:2048 * sl + 2048],
                            in_=d_wv[vg, kg]))
            kk.seg("gpsimd", f_gp2)

            base_svp = s_vp.n
            csp_v = c_spv.n

            def f_tens2(e, en):
                vbanks = (bA, bB, bC, bD)
                W(e, en, c_cg, tok["cg_all"])
                W(e, en, v_ms, 1)
                for vg in range(4):
                    for ci, b in enumerate(vbanks):
                        b.begin(e, en, rel=(s_vp, base_svp + (vg - 1) * 4 + ci + 1)
                                if vg >= 1 else None)
                    for c in range(NCHG):
                        e.matmul(vbanks[c].ps[:, :], ones_row[0:1, 0:128],
                                 bv_row[0:1, vg * 512:(vg + 1) * 512],
                                 start=True, stop=False)
                    for kdi in range(KD):
                        ti = g * 16 + vg * 4 + kdi // 4
                        st_wv.wait_tile(e, en, ti)
                        sl = 2 + (vg * 4 + kdi // 4) % 2
                        wsl = wstr[:, 2048 * sl + (kdi % 4) * 512:
                                   2048 * sl + (kdi % 4) * 512 + 512]
                        for c in range(NCHG):
                            inst = e.matmul(
                                vbanks[c].ps[:, :],
                                h_sb[:, kdi * GT + c * CH: kdi * GT + c * CH + CH],
                                wsl, start=False,
                                stop=(kdi == KD - 1))
                        if kdi % 4 == 3:
                            t_vk.inc(inst, 1)
            kk.seg("tensor", f_tens2)

            def f_scal2(e, en):
                vbanks = (bA, bB, bC, bD)
                W(e, en, c_spv, csp_v)
                for vg in range(4):
                    W(e, en, t_vk, g * 16 + (vg + 1) * 4)
                    for c in range(NCHG):
                        inst = e.activation(
                            v_sb[:, c * DM + vg * 512: c * DM + vg * 512 + 512],
                            vbanks[c].ps[:, :], AF.Identity)
                        s_vp.inc(inst, 1)
                        vbanks[c].release_at(s_vp, s_vp.n)
            kk.seg("scalar", f_scal2)

            def f_sync2(e, en):
                W(e, en, s_vp, base_svp + 16)
                c_spv.inc(e.dma_start(out=d_vsp.ap()[:, g * 8192:(g + 1) * 8192],
                                      in_=v_sb[:, :]), 16)
            kk.seg("sync", f_sync2)

            if g == NG - 1:
                def f_hreload(e, en):
                    W(e, en, t_kp, NG * KD)
                    W(e, en, t_vk, NG * 16)
                    W(e, en, c_sph, NG * 16)
                    c_l2h.inc(e.dma_start(out=h_sb[:, :],
                                          in_=d_hsp.ap()[:, 0:8192]), 16)
                kk.seg("sync", f_hreload)

            for c in range(NCHG):
                cg = g * NCHG + c
                csp_aw = c_spa.n

                def f_tens3(e, en, c=c, cg=cg):
                    W(e, en, s_kp, (g + 1) * KD)
                    W(e, en, s_vp, (g + 1) * 16)
                    W(e, en, c_cg, tok["cg_all"])
                    abanks = (bE, bF, bG)
                    for h in range(H):
                        ai = cg * H + h
                        bH.begin(e, en, rel=(s_kt, ai) if ai >= 1 else None)
                        inst = e.transpose(
                            psH[:, 0:128],
                            k_sb[:, h * GT + c * CH: h * GT + c * CH + CH],
                            id_bf[:, :])
                        t_kt.inc(inst, 1)
                        W(e, en, s_kt, cg * H + h + 1)
                        b = abanks[h % 3]
                        b.begin(e, en, rel=(v_su, ai - 2) if ai >= 3 else None)
                        inst = e.matmul(b.ps[:, 0:128], ktil[:, :],
                                        v_sb[:, c * DM + h * CH: c * DM + h * CH + CH],
                                        start=True, stop=True)
                        t_am.inc(inst, 1)
                kk.seg("tensor", f_tens3)

                def f_scal3(e, en, c=c, cg=cg, csp_aw=csp_aw):
                    W(e, en, c_k, tok["ck_all"])
                    W(e, en, c_spa, csp_aw)
                    abanks = (bE, bF, bG)
                    for h in range(H):
                        W(e, en, t_kt, cg * H + h + 1)
                        inst = e.activation(ktil[:, :], psH[:, 0:128], AF.Identity,
                                            scale=dk_sb[:, h:h + 1])
                        s_kt.inc(inst, 1)
                        bH.release_at(s_kt, s_kt.n)
                        W(e, en, t_am, cg * H + h + 1)
                        inst = e.activation(awin[:, h * CH:(h + 1) * CH],
                                            abanks[h % 3].ps[:, 0:128], AF.Identity)
                        s_ae.inc(inst, 1)
                kk.seg("scalar", f_scal3)

                def f_vec3(e, en, c=c, cg=cg):
                    abanks = (bE, bF, bG)
                    for h in range(H):
                        W(e, en, t_am, cg * H + h + 1)
                        W(e, en, s_ae, cg * H + h + 1)
                        b = abanks[h % 3]
                        inst = e.scalar_tensor_tensor(
                            S_sb[:, h * CH:(h + 1) * CH],
                            S_sb[:, h * CH:(h + 1) * CH], lamc[h],
                            b.ps[:, 0:128], AX.mult, AX.add)
                        v_su.inc(inst, 1)
                        b.release_at(v_su, v_su.n)
                kk.seg("vector", f_vec3)

                def f_sync3(e, en, c=c, cg=cg):
                    W(e, en, s_ae, (cg + 1) * H)
                    c_spa.inc(e.dma_start(
                        out=d_asp.ap()[:, cg * 2048:(cg + 1) * 2048],
                        in_=awin[:, :]), 16)
                kk.seg("sync", f_sync3)

        ln_stats()
        for g in range(NG):
            h_pass(g, sa_base=0, ca_base=16)

            def f_hspill(e, en, g=g, sh=None):
                W(e, en, s_h, s_h.n)
                c_sph.inc(e.dma_start(out=d_hsp.ap()[:, g * 8192:(g + 1) * 8192],
                                      in_=h_sb[:, :]), 16)
            kk.seg("sync", f_hspill)
            kv_group(g)

        def f_kvreload(e, en):
            W(e, en, t_kt, NCH * H)
            W(e, en, c_spk, NG * 16)
            c_l2k.inc(e.dma_start(out=k_sb[:, :], in_=d_ksp.ap()[:, 0:8192]), 16)
            W(e, en, t_am, NCH * H)
            W(e, en, c_spv, NG * 16)
            c_l2v.inc(e.dma_start(out=v_sb[:, :], in_=d_vsp.ap()[:, 0:8192]), 16)
        kk.seg("sync", f_kvreload)

        def f_wqpre(e, en):
            guard_streams(e, en, st_wq)
            for f in range(2):
                st_wq.load(e, en, lambda f=f: e.dma_start(
                    out=wstr[:, (f % 2) * 2048:(f % 2) * 2048 + 2048],
                    in_=d_wq[f]))
        kk.seg("gpsimd", f_wqpre)

        def f_gcast(e, en):
            W(e, en, v_su, NCH * H)
            s_gc.inc(e.activation(Sbf[:, :], S_sb[:, :], AF.Identity), 1)
        kk.seg("scalar", f_gcast)

        def f_gout(e, en):
            W(e, en, s_gc, 1)
            c_gi.inc(e.dma_start(out=d_gci.ap(), in_=Sbf[:, :]), 16)
        kk.seg("sync", f_gout)

        def f_gag(e, en):
            W(e, en, c_gi, 16)
            inst = e.collective_compute(
                "AllGather", AX.bypass, replica_groups=[list(range(NC))],
                ins=[d_gci.ap().opt()], outs=[d_gco.ap().opt()])
            c_cc.inc(inst, 1)
        kk.seg("gpsimd", f_gag)

        def f_gwin(e, en):
            W(e, en, c_cc, 2)
            for h in range(H):
                if h >= 2:
                    W(e, en, v_cb, (h - 1) * NC)
                (c_gw0, c_gw1)[h % 2].inc(e.dma_start(
                    out=gall[:, (h % 2) * 1024:(h % 2) * 1024 + 1024]
                    .rearrange("p (r c) -> p r c", r=NC),
                    in_=d_gco.ap().transpose([1, 0, 2])[:, :, h * CH:(h + 1) * CH]), 16)
        kk.seg("sync", f_gwin)

        def f_comb(e, en):
            W(e, en, c_k, tok["ck_all"])
            for h in range(H):
                W(e, en, (c_gw0, c_gw1)[h % 2], 16 * (h // 2 + 1))
                gw = gall[:, (h % 2) * 1024:(h % 2) * 1024 + 1024]
                ss = S_sb[:, h * CH:(h + 1) * CH]
                inst = e.tensor_scalar(ss, gw[:, 0:128], wc_sb[:, h:h + 1],
                                       None, AX.mult)
                v_cb.inc(inst, 1)
                for j in range(1, NC):
                    inst = e.scalar_tensor_tensor(
                        ss, gw[:, j * CH:(j + 1) * CH],
                        wc_sb[:, j * H + h:j * H + h + 1], ss, AX.mult, AX.add)
                    v_cb.inc(inst, 1)
        kk.seg("vector", f_comb)

        def f_scast0(e, en):
            for h in range(H):
                W(e, en, v_cb, (h + 1) * NC)
                s_cast.inc(e.activation(Sbf[:, h * CH:(h + 1) * CH],
                                        S_sb[:, h * CH:(h + 1) * CH],
                                        AF.Identity), 1)
        kk.seg("scalar", f_scast0)

        scan = {"sc2": 0, "ia": 0}

        def pass2_group(g):
            spa_all = c_spa.n
            sph_all = c_sph.n
            spk_all = c_spk.n
            spv_all = c_spv.n
            tqp_pre = t_qp.n
            tsc_pre = scan["sc2"]
            tia_pre = scan["ia"]

            def f_ld(e, en):
                W(e, en, t_qp, tqp_pre)
                W(e, en, t_sc2, tsc_pre)
                W(e, en, t_ia, tia_pre)
                c_l2h.inc(e.dma_start(out=h_sb[:, :],
                                      in_=d_hsp.ap()[:, g * 8192:(g + 1) * 8192]), 16)
                c_l2k.inc(e.dma_start(out=k_sb[:, :],
                                      in_=d_ksp.ap()[:, g * 8192:(g + 1) * 8192]), 16)
                c_l2v.inc(e.dma_start(out=v_sb[:, :],
                                      in_=d_vsp.ap()[:, g * 8192:(g + 1) * 8192]), 16)
            if g > 0:
                kk.seg("sync", f_ld)

            def f_gp(e, en):
                guard_streams(e, en, st_wq)
                for f in range(2 if g == 0 else 0, KD):
                    st_wq.load(e, en, lambda f=f: e.dma_start(
                        out=wstr[:, (f % 2) * 2048:(f % 2) * 2048 + 2048],
                        in_=d_wq[f]))
            kk.seg("gpsimd", f_gp)

            def f_tens(e, en):
                W(e, en, c_l2h, (g + 1) * 16)
                for f in range(KD):
                    ti = g * KD + f
                    st_wq.wait_tile(e, en, ti)
                    b = (bE, bF)[f % 2]
                    b.begin(e, en, rel=(s_qp, g * KD + f - 1) if f >= 2 else None)
                    sl = wstr[:, (f % 2) * 2048:(f % 2) * 2048 + 2048]
                    for kdi in range(KD):
                        inst = e.matmul(b.ps[:, :], sl[:, kdi * 128:(kdi + 1) * 128],
                                        h_sb[:, kdi * GT:(kdi + 1) * GT],
                                        start=(kdi == 0), stop=(kdi == KD - 1))
                    t_qp.inc(inst, 1)
            kk.seg("tensor", f_tens)

            def f_scal(e, en):
                W(e, en, c_k, tok["ck_all"])
                for f in range(KD):
                    W(e, en, t_qp, g * KD + f + 1)
                    b = (bE, bF)[f % 2]
                    inst = e.activation(q_sb[:, f * GT:(f + 1) * GT], b.ps[:, :],
                                        AF.Identity, bias=bq_sb[:, f:f + 1])
                    s_qp.inc(inst, 1)
                    b.release_at(s_qp, s_qp.n)
            kk.seg("scalar", f_scal)

            for c in range(NCHG):
                cg = g * NCHG + c

                def f_aw(e, en, c=c, cg=cg, spa_all=spa_all):
                    if cg == 0:
                        W(e, en, c_spa, NCH * 16)
                    if cg == 1:
                        W(e, en, v_cb, H * NC)
                    if cg >= 2:
                        W(e, en, v_su, NCH * H + (cg - 1) * H)
                    dst = awin if cg % 2 == 0 else gall[:, 0:2048]
                    (c_aw0, c_aw1)[cg % 2].inc(e.dma_start(
                        out=dst[:, :],
                        in_=d_asp.ap()[:, cg * 2048:(cg + 1) * 2048]), 16)
                kk.seg("sync", f_aw)

                def f_tens2(e, en, c=c, cg=cg, g=g):
                    W(e, en, s_qp, (g + 1) * KD)
                    W(e, en, c_l2k, (g + 1) * 16)
                    W(e, en, c_l2v, (g + 1) * 16)
                    for h in range(H):
                        si = cg * H + h
                        bC.begin(e, en, rel=(v_mk, si) if si >= 1 else None)
                        inst = e.matmul(
                            psC[:, 0:128],
                            k_sb[:, h * GT + c * CH: h * GT + c * CH + CH],
                            q_sb[:, h * GT + c * CH: h * GT + c * CH + CH],
                            start=True, stop=True)
                        t_sc2.inc(inst, 1)
                        W(e, en, s_cast, (h + 1) if cg == 0 else (H + cg))
                        bD.begin(e, en, rel=(v_mg, si) if si >= 1 else None)
                        e.matmul(psD[:, 0:128], Sbf[:, h * CH:(h + 1) * CH],
                                 q_sb[:, h * GT + c * CH: h * GT + c * CH + CH],
                                 start=True, stop=True)
                        W(e, en, v_mk, cg * H + h + 1)
                        bG.begin(e, en, rel=(v_mg, si) if si >= 1 else None)
                        inst = e.matmul(
                            psG[:, 0:128],
                            v_sb[:, c * DM + h * CH: c * DM + h * CH + CH],
                            scT[:, :], start=True, stop=True)
                        t_ia.inc(inst, 1)
                    scan["sc2"] = t_sc2.n
                    scan["ia"] = t_ia.n
                kk.seg("tensor", f_tens2)

                def f_vec2(e, en, c=c, cg=cg):
                    W(e, en, c_cg, tok["cg_all"])
                    for h in range(H):
                        W(e, en, t_sc2, cg * H + h + 1)
                        inst = e.tensor_mul(scT[:, :], psC[:, 0:128],
                                            dm_bf[:, h * CH:(h + 1) * CH])
                        v_mk.inc(inst, 1)
                        bC.release_at(v_mk, v_mk.n)
                        W(e, en, t_ia, cg * H + h + 1)
                        mg = stg[:, 512:640]
                        e.tensor_mul(mg, psD[:, 0:128], dq_bf[:, h * CH:(h + 1) * CH])
                        inst = e.tensor_add(
                            attn_sb[:, h * GT + c * CH: h * GT + c * CH + CH],
                            mg, psG[:, 0:128])
                        v_mg.inc(inst, 1)
                        bD.release_at(v_mg, v_mg.n)
                        bG.release_at(v_mg, v_mg.n)
                        W(e, en, (c_aw0, c_aw1)[cg % 2], 16 * (cg // 2 + 1))
                        src = awin if cg % 2 == 0 else gall[:, 0:2048]
                        inst = e.scalar_tensor_tensor(
                            S_sb[:, h * CH:(h + 1) * CH],
                            S_sb[:, h * CH:(h + 1) * CH], lamc[h],
                            src[:, h * CH:(h + 1) * CH], AX.mult, AX.add)
                        v_su.inc(inst, 1)
                kk.seg("vector", f_vec2)

                def f_scast(e, en, c=c, cg=cg):
                    W(e, en, v_su, NCH * H + (cg + 1) * H)
                    W(e, en, t_ia, (cg + 1) * H)
                    s_cast.inc(e.activation(Sbf[:, :], S_sb[:, :], AF.Identity), 1)
                kk.seg("scalar", f_scast)

            def f_gp2(e, en):
                guard_streams(e, en, st_wo)
                for f in range(KD):
                    sl = 2 + f % 2
                    st_wo.load(e, en, lambda f=f, sl=sl: e.dma_start(
                        out=wstr[:, 2048 * sl:2048 * sl + 2048], in_=d_wo[f]))
            kk.seg("gpsimd", f_gp2)

            sop_pre = s_op.n
            base_vr = v_rs.n
            base_top = t_op.n

            def f_tens3(e, en):
                W(e, en, v_mg, (g + 1) * NCHG * H)
                for f in range(KD):
                    ti = g * KD + f
                    st_wo.wait_tile(e, en, ti)
                    b = (bE, bF)[f % 2]
                    b.begin(e, en, rel=(s_op, sop_pre + f - 1) if f >= 2 else None)
                    sl = 2 + f % 2
                    slc = wstr[:, 2048 * sl:2048 * sl + 2048]
                    for kdi in range(KD):
                        inst = e.matmul(b.ps[:, :], slc[:, kdi * 128:(kdi + 1) * 128],
                                        attn_sb[:, kdi * GT:(kdi + 1) * GT],
                                        start=(kdi == 0), stop=(kdi == KD - 1))
                    t_op.inc(inst, 1)
            kk.seg("tensor", f_tens3)

            def f_scal3(e, en):
                for f in range(KD):
                    W(e, en, t_op, base_top + f + 1)
                    if f >= 2:
                        W(e, en, v_rs, base_vr + f - 1)
                    b = (bE, bF)[f % 2]
                    tb = tb_slots[f % 2][:, :]
                    inst = e.activation(tb, b.ps[:, :], AF.Identity,
                                        bias=bga_sb[:, f:f + 1],
                                        scale=modsb[:, 32 + f:33 + f])
                    s_op.inc(inst, 1)
                    b.release_at(s_op, s_op.n)
            kk.seg("scalar", f_scal3)

            def f_vec3(e, en):
                for f in range(KD):
                    W(e, en, s_op, sop_pre + f + 1)
                    xs = x_sb[:, f * TLOC + g * GT: f * TLOC + g * GT + GT]
                    tb = tb_slots[f % 2][:, :]
                    inst = e.tensor_add(xs, xs, tb)
                    v_rs.inc(inst, 1)
            kk.seg("vector", f_vec3)

        for g in range(NG):
            pass2_group(g)

        ln_stats()

        def ffn_group(g):
            h_pass(g, sa_base=48, ca_base=64)

            base_cs = t_cs.n
            sh_full = s_h.n

            def f_cs(e, en):
                bG.begin(e, en)
                for kdi in range(KD):
                    W(e, en, s_h, sh_full - KD + kdi + 1)
                    inst = e.matmul(psG[0:1, :], onec[:, :],
                                    h_sb[:, kdi * GT:(kdi + 1) * GT],
                                    start=(kdi == 0), stop=(kdi == KD - 1))
                t_cs.inc(inst, 1)
            kk.seg("tensor", f_cs)

            base_vcs = v_cs.n

            def f_csv(e, en):
                W(e, en, t_cs, base_cs + 1)
                inst = e.tensor_scalar_mul(ch_row, psG[0:1, :], -ZP)
                v_cs.inc(inst, 1)
                bG.release_at(v_cs, v_cs.n)
            kk.seg("vector", f_csv)

            def f_chb(e, en):
                W(e, en, v_cs, base_vcs + 1)
                bA.begin(e, en)
                inst = e.matmul(psA[:, :], ones_row[0:1, 0:128], ch_row,
                                start=True, stop=True)
                t_chb.inc(inst, 1)
            kk.seg("tensor", f_chb)

            def f_chbs(e, en):
                W(e, en, t_chb, g + 1)
                W(e, en, v_zc, g * 2 * NJ)
                inst = e.activation(chb_sb[:, :], psA[:, :], AF.Identity)
                s_chb.inc(inst, 1)
                bA.release_at(s_chb, s_chb.n)
            kk.seg("scalar", f_chbs)

            def f_gp(e, en):
                guard_streams(e, en, st_wg)
                for j in range(NJ):
                    s0 = (2 * j) % 4
                    s1 = (2 * j + 1) % 4
                    st_wg.load(e, en, lambda j=j, s0=s0: e.dma_start(
                        out=wstr[:, 2048 * s0:2048 * s0 + 2048], in_=d_wg[j]))
                    st_wg.load(e, en, lambda j=j, s1=s1: e.dma_start(
                        out=wstr[:, 2048 * s1:2048 * s1 + 2048], in_=d_wu[j]))
            kk.seg("gpsimd", f_gp)

            base_tff = t_ff.n
            base_sff = s_ff.n
            base_sfg = s_fg.n
            base_vff = v_ff.n

            base_vzc = v_zc.n

            def f_tens(e, en):
                def colsum(jj):
                    W(e, en, v_ff, base_vff + jj + 1)
                    if jj == 0:
                        bG.begin(e, en)
                    inst = e.matmul(psG[0:1, :], onec[:, :],
                                    gp_sb[:, jj * 512:(jj + 1) * 512],
                                    start=(jj == 0), stop=(jj == NJ - 1))
                    if jj == NJ - 1:
                        t_cs.inc(inst, 1)
                for j in range(NJ):
                    tg = g * 2 * NJ + 2 * j
                    st_wg.wait_tile(e, en, tg)
                    st_wg.wait_tile(e, en, tg + 1)
                    for bi, b in enumerate(((bC, bD), (bE, bF))[j % 2]):
                        relch = ((s_fg, base_sfg + j - 1) if bi == 0
                                 else (s_ff, base_sff + j - 1))
                        b.begin(e, en, rel=relch if j >= 2 else None)
                        sl = wstr[:, 2048 * ((2 * j + bi) % 4):
                                  2048 * ((2 * j + bi) % 4) + 2048]
                        for kdi in range(KD):
                            if j == 0:
                                W(e, en, s_h, sh_full - KD + kdi + 1)
                            inst = e.matmul(b.ps[:, :], sl[:, kdi * 128:(kdi + 1) * 128],
                                            h_sb[:, kdi * GT:(kdi + 1) * GT],
                                            start=(kdi == 0), stop=(kdi == KD - 1))
                        if bi == 1:
                            t_ff.inc(inst, 1)
                    if j >= 2:
                        colsum(j - 2)
                colsum(NJ - 2)
                colsum(NJ - 1)
            kk.seg("tensor", f_tens)

            def f_scal(e, en):
                W(e, en, c_k, tok["ck_all"])
                for j in range(NJ):
                    bg, bu = ((bC, bD), (bE, bF))[j % 2]
                    W(e, en, v_zc, base_vzc + 2 * j + 1)
                    inst = e.activation(gp_sb[:, j * 512:(j + 1) * 512], bg.ps[:, :],
                                        AF.Silu, scale=sg_sb[:, j:j + 1])
                    s_fg.inc(inst, 1)
                    bg.release_at(s_fg, s_fg.n)
                    if j > 0:
                        W(e, en, v_ff, base_vff + j)
                    W(e, en, v_zc, base_vzc + 2 * j + 2)
                    inst = e.activation(stg[:, 0:512], bu.ps[:, :], AF.Identity,
                                        scale=su_sb[:, j:j + 1])
                    s_ff.inc(inst, 1)
                    bu.release_at(s_ff, s_ff.n)
            kk.seg("scalar", f_scal)

            def f_vec(e, en):
                for j in range(NJ):
                    bg, bu = ((bC, bD), (bE, bF))[j % 2]
                    W(e, en, t_ff, base_tff + j + 1)
                    W(e, en, s_chb, g + 1)
                    v_zc.inc(e.tensor_add(bg.ps[:, :], bg.ps[:, :], chb_sb[:, :]), 1)
                    v_zc.inc(e.tensor_add(bu.ps[:, :], bu.ps[:, :], chb_sb[:, :]), 1)
                    W(e, en, s_ff, base_sff + j + 1)
                    W(e, en, s_fg, base_sfg + j + 1)
                    gpj = gp_sb[:, j * 512:(j + 1) * 512]
                    inst = e.tensor_mul(gpj, stg[:, 0:512], gpj)
                    v_ff.inc(inst, 1)
            kk.seg("vector", f_vec)

            base_cs2 = t_cs.n

            def f_cs2v(e, en):
                W(e, en, t_cs, base_cs2)
                inst = e.tensor_scalar_mul(cp_row, psG[0:1, :], -ZP)
                v_cs.inc(inst, 1)
                bG.release_at(v_cs, v_cs.n)
            kk.seg("vector", f_cs2v)

            def f_gp2(e, en):
                guard_streams(e, en, st_wd)
                for m in range(16):
                    for qq in range(4):
                        sl = (4 * m + qq) % 4
                        st_wd.load(e, en, lambda m=m, qq=qq, sl=sl: e.dma_start(
                            out=wstr[:, 2048 * sl:2048 * sl + 1408],
                            in_=d_wd[m][:, qq * 1408:(qq + 1) * 1408]))
            kk.seg("gpsimd", f_gp2)

            base_top = t_op.n
            base_sop = s_op.n
            base_vrs = v_rs.n
            vcs_now = v_cs.n

            def f_tens2(e, en):
                W(e, en, v_cs, vcs_now)
                for m in range(16):
                    b = (bE, bF)[m % 2]
                    b.begin(e, en, rel=(s_op, base_sop + m - 1) if m >= 2 else None)
                    e.matmul(b.ps[:, :], ones_row[0:1, 0:128], cp_row,
                             start=True, stop=False)
                    for qq in range(4):
                        ti = g * 64 + m * 4 + qq
                        st_wd.wait_tile(e, en, ti)
                        sl = wstr[:, 2048 * ((4 * m + qq) % 4):
                                  2048 * ((4 * m + qq) % 4) + 1408]
                        for hh in range(11):
                            hc = qq * 11 + hh
                            inst = e.matmul(b.ps[:, :], sl[:, hh * 128:(hh + 1) * 128],
                                            gp_sb[:, hc * 512:(hc + 1) * 512],
                                            start=False,
                                            stop=(qq == 3 and hh == 10))
                        if qq < 3:
                            t_dq3.inc(inst, 1)
                    t_op.inc(inst, 1)
            kk.seg("tensor", f_tens2)

            def f_scal2(e, en):
                for m in range(16):
                    W(e, en, t_op, base_top + m + 1)
                    if m >= 2:
                        W(e, en, v_rs, base_vrs + m - 1)
                    b = (bE, bF)[m % 2]
                    tb = tb_slots[m % 2][:, :]
                    inst = e.activation(tb, b.ps[:, :], AF.Identity,
                                        scale=sdgb_sb[:, m:m + 1])
                    s_op.inc(inst, 1)
                    b.release_at(s_op, s_op.n)
            kk.seg("scalar", f_scal2)

            def f_vec2(e, en):
                for m in range(16):
                    W(e, en, s_op, base_sop + m + 1)
                    xs = x_sb[:, m * TLOC + g * GT: m * TLOC + g * GT + GT]
                    tb = tb_slots[m % 2][:, :]
                    inst = e.tensor_add(xs, xs, tb)
                    v_rs.inc(inst, 1)
            kk.seg("vector", f_vec2)

            def f_ostream(e, en):
                for m in range(16):
                    W(e, en, v_rs, base_vrs + m + 1)
                    c_out.inc(e.dma_start(
                        out=d_out[:, m * TLOC + g * GT: m * TLOC + g * GT + GT],
                        in_=x_sb[:, m * TLOC + g * GT: m * TLOC + g * GT + GT]), 16)
            kk.seg("sync", f_ostream)

        for g in range(NG):
            ffn_group(g)

        vr_final = v_rs.n

        def f_out(e, en):
            W(e, en, v_rs, vr_final)
            if DEBUG_TAPS:
                c_out.inc(e.dma_start(out=dbg["dbg_modsb"], in_=modsb[:, :]), 16)
                c_out.inc(e.dma_start(out=dbg["dbg_rows"], in_=rows[0:1, :]), 16)
                c_out.inc(e.dma_start(out=dbg["dbg_S"], in_=S_sb[:, :]), 16)
                c_out.inc(e.dma_start(out=dbg["dbg_attn"], in_=attn_sb[:, :]), 16)
                c_out.inc(e.dma_start(out=dbg["dbg_hsp"], in_=d_hsp.ap()), 16)
                c_out.inc(e.dma_start(out=dbg["dbg_ksp"], in_=d_ksp.ap()), 16)
                c_out.inc(e.dma_start(out=dbg["dbg_vsp"], in_=d_vsp.ap()), 16)
                c_out.inc(e.dma_start(out=dbg["dbg_asp"], in_=d_asp.ap()), 16)
                c_out.inc(e.dma_start(out=dbg["dbg_gco"], in_=d_gco.ap()), 16)
            W(e, en, c_out, c_out.n)
        kk.seg("sync", f_out)

    bad = [(en, ch.sem.name, v, ch.n) for (en, ch, v) in kk.audit if v > ch.n]
    assert not bad, f"waits exceeding final counts: {bad[:10]}"
    return nc




def kernel(L0_attn_x, L0_attn_w_qkv, L0_attn_b_qkv, L0_attn_w_out, L0_attn_b_out,
           L0_ffn_wq_gate, L0_ffn_s_gate, L0_ffn_wq_up, L0_ffn_s_up,
           L0_ffn_wq_down, L0_ffn_s_down, L0_c, L0_wmod):
    x = np.asarray(L0_attn_x, dtype=np.float32)
    w_qkv = np.asarray(L0_attn_w_qkv, dtype=np.float32)
    b_qkv = np.asarray(L0_attn_b_qkv, dtype=np.float32)
    w_out = np.asarray(L0_attn_w_out, dtype=np.float32)
    b_out = np.asarray(L0_attn_b_out, dtype=np.float32)
    s_g = np.asarray(L0_ffn_s_gate, dtype=np.float32)
    s_u = np.asarray(L0_ffn_s_up, dtype=np.float32)
    s_d = np.asarray(L0_ffn_s_down, dtype=np.float32)
    cvec = np.asarray(L0_c, dtype=np.float32)
    wmod = np.asarray(L0_wmod, dtype=np.float32)

    def u8(a):
        a = np.ascontiguousarray(np.asarray(a, dtype=np.int32))
        return np.ascontiguousarray(a.view(np.uint8).reshape(a.shape + (4,))[..., 0])

    wg8 = u8(L0_ffn_wq_gate)
    wu8_ = u8(L0_ffn_wq_up)
    wd8 = u8(L0_ffn_wq_down)

    dm32, dq32, dk32, _, lamc = _host_consts(0)
    ident = np.eye(128, dtype=np.float32)

    wq_blob = _lhsT_blob(np.ascontiguousarray(w_qkv[0:2048].T), 16)
    wk_blob = _lhsT_blob(np.ascontiguousarray(w_qkv[2048:4096].T), 16)
    wv_blob = _wv_blob(np.ascontiguousarray(w_qkv[4096:6144].T))
    wo_blob = _lhsT_blob(np.ascontiguousarray(w_out.T), 16)
    wg_blob = _wg_blob(np.ascontiguousarray(wg8.T))
    wu_blob = _wg_blob(np.ascontiguousarray(wu8_.T))
    wd_blob = _wd_blob(np.ascontiguousarray(wd8.T))

    bq_c = _col16(b_qkv[0:2048])
    bk_c = _col16(b_qkv[2048:4096])
    bv_r = np.ascontiguousarray(b_qkv[4096:6144]).reshape(1, DM)
    bo_c = _col16(b_out)
    sg_c = _col44(s_g)
    su_c = _col44(s_u)
    sd_c = _col16(s_d)
    cc_c = _col16(cvec[0])

    nc = build_program(lamc)

    in_maps = []
    for core in range(NC):
        _, _, _, wc32, _ = _host_consts(core)
        in_maps.append({
            "x": _x_blob(x, core),
            "dmask": dm32, "dqt": dq32, "ident": ident, "dk": dk32,
            "wcomb": wc32, "cvec": cc_c,
            "bq": bq_c, "bk": bk_c, "bv": bv_r, "bo": bo_c,
            "sg": sg_c, "su": su_c, "sd": sd_c,
            "wq": wq_blob, "wk": wk_blob, "wv": wv_blob, "wo": wo_blob,
            "wg": wg_blob, "wu": wu_blob, "wd": wd_blob,
            "wm": _wm_blob(wmod, core),
        })

    res = run_bass_kernel_spmd(nc, in_maps, core_ids=list(range(NC)))

    out = np.empty((1, SEQ, DM), dtype=np.float32)
    for core in range(NC):
        blob = res.results[core]["out"]
        out[0, core * TLOC:(core + 1) * TLOC, :] = (
            np.asarray(blob).reshape(128, KD, TLOC).transpose(2, 1, 0)
            .reshape(TLOC, DM))
    return out
